# version 07
# baseline (speedup 1.0000x reference)
import numpy as np
import ml_dtypes
import concourse.bass as bass
import concourse.mybir as mybir
from concourse.bass_utils import run_bass_kernel_spmd
from concourse.alu_op_type import AluOpType as OP

F32 = mybir.dt.float32
BF16 = mybir.dt.bfloat16
AF = mybir.ActivationFunctionType
AX = mybir.AxisListType

TH = 2048
D = 2048
H = 8192
NT = TH // 128
NP = TH // 128
NC = 64
NDT = D // 128
RS = float(1.0 / np.sqrt(8192.0))

ENGINES = ("tensor", "vector", "scalar", "gpsimd", "sync")


class Sem:
    __slots__ = ("h", "n", "name")

    def __init__(self, h, name):
        self.h = h
        self.n = 0
        self.name = name


class K:
    def __init__(self, nc):
        self.nc = nc
        self.items = {e: [] for e in ENGINES}
        self.sem_ctxs = []
        self.sems = []

    def sem(self, name):
        ctx = self.nc.semaphore(name)
        h = ctx.__enter__()
        self.sem_ctxs.append(ctx)
        s = Sem(h, name)
        self.sems.append(s)
        return s

    def wait(self, eng, sem, val):
        assert val <= sem.n, f"wait on {sem.name} for {val} but only {sem.n} emitted"
        if val > 0:
            self.items[eng].append(("w", sem, val))

    def op(self, eng, fn, inc=None, amt=1):
        self.items[eng].append(("o", fn, inc, amt))
        if inc is not None:
            inc.n += amt
            return inc.n
        return None

    def dma(self, eng, dst, src, sem, **kw):
        return self.op(eng, lambda e: e.dma_start(dst, src, **kw), inc=sem, amt=16)

    def dmat(self, eng, dst, src, sem):
        return self.op(eng, lambda e: e.dma_start_transpose(dst, src), inc=sem, amt=16)

    def barrier(self):
        snap = [(s, s.n) for s in self.sems if s.n > 0]
        for e in ENGINES:
            for s, v in snap:
                self.items[e].append(("w", s, v))

    def build(self, block):
        for eng in ENGINES:
            dec = getattr(block, eng)
            items = self.items[eng]

            def make(items=items):
                def body(h):
                    for it in items:
                        if it[0] == "w":
                            h.wait_ge(it[1].h, it[2])
                        else:
                            inst = it[1](h)
                            if it[2] is not None:
                                inst.then_inc(it[2].h, it[3])
                return body
            dec(make())

    def close_sems(self):
        for ctx in reversed(self.sem_ctxs):
            ctx.__exit__(None, None, None)


class Alloc:
    def __init__(self, nc):
        self.nc = nc
        self.stack = []

    def sb(self, name, shape, dtype):
        ctx = self.nc.sbuf_tensor(name, shape, dtype)
        h = ctx.__enter__()
        self.stack.append((name, ctx))
        return h

    def free_from(self, mark):
        while len(self.stack) > mark:
            name, ctx = self.stack.pop()
            ctx.__exit__(None, None, None)

    def mark(self):
        return len(self.stack)


def build_nc():
    nc = bass.Bass("TRN2", target_bir_lowering=False)

    xo_d = nc.dram_tensor("xo", [TH, D], F32, kind="ExternalInput").ap()
    xp_d = nc.dram_tensor("xp", [TH, D], F32, kind="ExternalInput").ap()
    wqk_d = nc.dram_tensor("wqk", [D, 128], F32, kind="ExternalInput").ap()
    wbd_d = nc.dram_tensor("wbd", [D, 128], F32, kind="ExternalInput").ap()
    wbb_d = nc.dram_tensor("wbb", [D, 128], F32, kind="ExternalInput").ap()
    wv_d = nc.dram_tensor("wv", [D, 128], F32, kind="ExternalInput").ap()
    wo_d = nc.dram_tensor("wo", [128, D], F32, kind="ExternalInput").ap()
    bb_d = nc.dram_tensor("bb", [64, 1], F32, kind="ExternalInput").ap()
    bd_d = nc.dram_tensor("bd", [64, 1], F32, kind="ExternalInput").ap()
    w1_d = nc.dram_tensor("w1", [H, D], F32, kind="ExternalInput").ap()
    w2_d = nc.dram_tensor("w2", [D, H], F32, kind="ExternalInput").ap()
    w3_d = nc.dram_tensor("w3", [H, D], F32, kind="ExternalInput").ap()
    hb1_d = nc.dram_tensor("hb1", [128, 128], BF16, kind="ExternalInput").ap()
    hb2_d = nc.dram_tensor("hb2", [128, 128], BF16, kind="ExternalInput").ap()
    msk_d = nc.dram_tensor("msk", [128, 128], BF16, kind="ExternalInput").ap()
    idn_d = nc.dram_tensor("idn", [128, 128], BF16, kind="ExternalInput").ap()

    out_d = nc.dram_tensor("out", [TH, D], F32, kind="ExternalOutput").ap()

    w2s = nc.dram_tensor("w2s", [D, H], BF16, kind="Internal").ap()
    w1ts = nc.dram_tensor("w1ts", [NC, 128, D], BF16, kind="Internal").ap()
    w3ts = nc.dram_tensor("w3ts", [NC, 128, D], BF16, kind="Internal").ap()
    x1s = nc.dram_tensor("x1s", [TH, D], F32, kind="Internal").ap()
    zbs = nc.dram_tensor("zbs", [128, NC, TH], BF16, kind="Internal").ap()
    zfs = nc.dram_tensor("zfs", [64, 128, TH], BF16, kind="Internal").ap()
    ffs_dT = nc.dram_tensor("ffs", [D, TH], BF16, kind="Internal").ap()

    al = Alloc(nc)

    idn = al.sb("idn_s", [128, 128], BF16)
    hb1 = al.sb("hb1_s", [128, 128], BF16)
    hb2 = al.sb("hb2_s", [128, 128], BF16)
    msk = al.sb("msk_s", [128, 128], BF16)
    stat = al.sb("stat_s", [128, 16], F32)
    sqs = al.sb("sqs_s", [128, D], BF16)
    xt = al.sb("xt_s", [128, 2 * D], F32)
    xnb = al.sb("xnb_s", [128, 2 * D], BF16)
    x1t = al.sb("x1t_s", [128, D], F32)
    epsb = al.sb("epsb_s", [128, 1], F32)

    aw_mark = al.mark()
    wqk = al.sb("wqk_s", [128, NDT * 128], BF16)
    wbd = al.sb("wbd_s", [128, NDT * 128], BF16)
    wbb = al.sb("wbb_s", [128, NDT * 128], BF16)
    wv = al.sb("wv_s", [128, NDT * 128], BF16)
    wo = al.sb("wo_s", [128, D], BF16)
    bias1 = al.sb("bias1_s", [128, 1], F32)
    bias2 = al.sb("bias2_s", [128, 1], F32)

    xn2T_mark = al.mark()
    xn2T = al.sb("xn2T_b", [128, NDT * TH], BF16)

    psum_ctxs = []

    def ps(name, shape, dtype):
        ctx = nc.psum_tensor(name, shape, dtype)
        h = ctx.__enter__()
        psum_ctxs.append(ctx)
        return h

    pA0 = ps("pA0", [128, 512], F32)
    pA1 = ps("pA1", [128, 512], F32)
    pT0 = ps("pT0", [128, 1024], BF16)
    pT1 = ps("pT1", [128, 1024], BF16)
    pM0 = ps("pM0", [128, 512], F32)
    pOT = ps("pOT", [128, 512], F32)
    pP = ps("pP", [128, 512], F32)
    pM1 = ps("pM1", [128, 512], F32)

    k = K(nc)
    s_cst = k.sem("s_cst")
    s_ld = k.sem("s_ld")
    s_w2c = k.sem("s_w2c")
    s_wl = [k.sem("s_wl0"), k.sem("s_wl1")]
    s_wp = [k.sem("s_wp0"), k.sem("s_wp1")]
    s_wpw = [k.sem("s_wpw0"), k.sem("s_wpw1")]
    s_xt = [k.sem("s_xt0"), k.sem("s_xt1")]
    s_x1w = k.sem("s_x1w")
    s_zbw = [k.sem("s_zbw0"), k.sem("s_zbw1")]
    s_gg = [k.sem(f"s_gg{i}") for i in range(4)]
    s_zfw = [k.sem(f"s_zfw{i}") for i in range(4)]
    s_zfl = [k.sem(f"s_zfl{i}") for i in range(4)]
    s_w2t = [k.sem("s_w2t0"), k.sem("s_w2t1")]
    s_ffw = k.sem("s_ffw")
    s_ffl = [k.sem("s_ffl0"), k.sem("s_ffl1")]
    s_t = k.sem("s_t")
    s_v = k.sem("s_v")
    s_a = k.sem("s_a")
    s_st = k.sem("s_st")
    s_out = k.sem("s_out")

    k.dma("sync", idn[:], idn_d, s_cst)
    k.dma("sync", hb1[:], hb1_d, s_cst)
    k.dma("sync", hb2[:], hb2_d, s_cst)
    k.dma("sync", msk[:], msk_d, s_cst)
    consts_ld = s_cst.n
    for dst, src in ((wqk, wqk_d), (wbd, wbd_d), (wbb, wbb_d), (wv, wv_d)):
        k.dma("gpsimd", dst[:], src.rearrange("(dt p) f -> p dt f", p=128), s_ld)
    k.dma("gpsimd", wo[:], wo_d, s_ld)
    k.dma("gpsimd", bias1[0:64, :], bb_d, s_ld)
    k.dma("gpsimd", bias1[64:128, :], bd_d, s_ld)
    k.dma("gpsimd", bias2[64:128, :], bb_d, s_ld)
    ld_small = s_ld.n
    for ch in range(32):
        k.dma("gpsimd", w2s[ch * 64:(ch + 1) * 64, :], w2_d[ch * 64:(ch + 1) * 64, :], s_w2c)
    w2cast_done = s_w2c.n

    k.wait("vector", s_ld, ld_small)
    k.op("vector", lambda v: v.tensor_scalar(bias1[0:64, :], bias1[0:64, :], -1.0, None, OP.mult))
    k.op("vector", lambda v: v.tensor_scalar(bias2[64:128, :], bias2[64:128, :], -1.0, None,
                                             OP.mult), inc=s_v)
    k.op("vector", lambda v: v.memset(epsb[:], 1e-5), inc=s_v)
    bias_ready = s_v.n
    k.wait("tensor", s_cst, consts_ld)

    amark = al.mark()
    xnT = al.sb("xnT_b", [128, NDT * 512], BF16)
    qkf = al.sb("qkf_b", [128, TH], BF16)
    nlb2 = al.sb("nlb2_b", [128, TH], F32)
    dd = al.sb("dd_b", [128, TH], BF16)
    vfm = al.sb("vfm_b", [128, TH], BF16)
    escr = al.sb("escr_b", [128, 512], F32)
    scum = al.sb("scum_b", [128, TH], F32)
    aa = al.sb("aa_b", [128, TH], BF16)
    qp = al.sb("qp_b", [128, TH], BF16)
    kpw = al.sb("kpw_b", [128, TH], BF16)
    kpf = al.sb("kpf_b", [128, TH], BF16)
    kptm = al.sb("kptm_b", [128, TH], BF16)
    kpf0 = al.sb("kpf0_b", [128, TH], BF16)
    vtm = al.sb("vtm_b", [128, TH], BF16)
    mtm = al.sb("mtm_b", [128, 256], BF16)
    acb = al.sb("acb_b", [64, 32], F32)
    Sf = al.sb("Sf_b", [64, 128], F32)
    Sb = al.sb("Sb_b", [64, 128], BF16)
    Stmp = al.sb("Stmp_b", [64, 128], F32)
    oT = al.sb("oT_b", [128, TH], BF16)
    att = scum

    k.op("vector", lambda v: v.memset(Sf[:], 0.0))
    k.op("vector", lambda v: v.memset(Sb[:], 0.0))
    k.op("vector", lambda v: v.memset(kpf[:], 0.0))
    k.op("vector", lambda v: v.memset(qp[:], 0.0))
    k.op("vector", lambda v: v.memset(kpf0[:], 0.0), inc=s_v)
    memset_done = s_v.n

    ln_state = {"slot": 0, "free": [(s_v, 0), (s_v, 0)]}

    def emit_ln_stats_apply(src_ap, dst_ap, dst_free=None):
        sl = ln_state["slot"]
        ln_state["slot"] ^= 1
        c0 = sl * 8

        def st_(i):
            return stat[:, c0 + i:c0 + i + 1]

        fsem, fval = ln_state["free"][sl]
        k.wait("vector", fsem, fval)
        k.op("vector", lambda v: v.reduce_sum(st_(0), src_ap, axis=AX.X), inc=s_v)
        rdone = s_v.n
        k.wait("scalar", fsem, fval)
        k.op("scalar", lambda s: s.activation(sqs[:, 0:D], src_ap, AF.Square,
                                              accum_out=st_(1)), inc=s_a)
        sq_done = s_a.n
        k.wait("vector", s_v, rdone)
        k.op("vector", lambda v: v.tensor_scalar(st_(2), st_(0), 1.0 / D, None, OP.mult),
             inc=s_v)
        k.wait("vector", s_a, sq_done)
        k.wait("vector", s_v, s_v.n)
        k.op("vector", lambda v: v.tensor_scalar(st_(3), st_(1), 1.0 / D, None, OP.mult))
        k.op("vector", lambda v: v.tensor_tensor(st_(4), st_(2), st_(2), OP.mult), inc=s_v)
        k.wait("vector", s_v, s_v.n)
        k.op("vector", lambda v: v.tensor_tensor(st_(4), st_(3), st_(4), OP.subtract),
             inc=s_v)
        var_done = s_v.n
        k.wait("scalar", s_v, var_done)
        k.op("scalar", lambda s: s.activation(st_(5), st_(4), AF.Ln, bias=epsb[:, 0:1]),
             inc=s_a)
        k.wait("scalar", s_a, s_a.n)
        k.op("scalar", lambda s: s.activation(st_(6), st_(5), AF.Exp, scale=-0.5), inc=s_a)
        rstd_done = s_a.n
        k.wait("vector", s_a, rstd_done)
        k.op("vector", lambda v: v.scalar_tensor_tensor(st_(7), st_(2), -1.0, st_(6),
                                                        OP.mult, OP.mult), inc=s_v)
        nmr_done = s_v.n
        k.wait("scalar", s_v, nmr_done)
        if dst_free is not None:
            k.wait("scalar", dst_free[0], dst_free[1])
        k.op("scalar", lambda s: s.activation(dst_ap, src_ap, AF.Identity,
                                              bias=st_(7), scale=st_(6)), inc=s_a)
        ln_state["free"][sl] = (s_a, s_a.n)
        return s_a.n

    tp_state = {"n": 0}
    tp_free = [(s_v, 0), (s_v, 0)]

    def emit_transpose(in_ap, out_ap, npart, evac_eng="vector"):
        par = tp_state["n"] % 2
        tp_state["n"] += 1
        bank = pT0 if par == 0 else pT1
        sem, val = tp_free[par]
        k.wait("tensor", sem, val)
        k.op("tensor", lambda t, b=bank, i=in_ap: t.transpose(b[0:npart, 0:128], i, idn[:]),
             inc=s_t)
        tdone = s_t.n
        if evac_eng == "vector":
            k.wait("vector", s_t, tdone)
            k.op("vector", lambda v, b=bank, o=out_ap: v.tensor_copy(o, b[0:npart, 0:128]),
                 inc=s_v)
            tp_free[par] = (s_v, s_v.n)
        else:
            k.wait("scalar", s_t, tdone)
            k.op("scalar", lambda s, b=bank, o=out_ap: s.copy(o, b[0:npart, 0:128]), inc=s_a)
            tp_free[par] = (s_a, s_a.n)
        return tp_free[par]

    w1r_ = w1_d.rearrange("(h1 j c) d -> c j h1 d", h1=64, j=2, c=64)
    w3r_ = w3_d.rearrange("(h1 j c) d -> c j h1 d", h1=64, j=2, c=64)
    wprep = {"written": {}, "next": 0, "proj_done": 0}

    def emit_wprep_block():
        c = wprep["next"]
        if c >= NC:
            return
        wprep["next"] += 1
        par = c % 2
        b1 = xnT[:, par * D:(par + 1) * D]
        b3 = xnT[:, 2 * D + par * D:2 * D + (par + 1) * D]
        if c < 2:
            k.wait("gpsimd", s_t, wprep["proj_done"])
        else:
            k.wait("gpsimd", s_wpw[par], wprep["written"][c - 2])
        k.dma("gpsimd", b1, w1r_[c], s_wp[par])
        k.dma("gpsimd", b3, w3r_[c], s_wp[par])
        wl = s_wp[par].n
        k.wait("tensor", s_wp[par], wl)
        for dt in range(NDT):
            emit_transpose(b1[:, dt * 128:(dt + 1) * 128], b1[:, dt * 128:(dt + 1) * 128],
                           128, evac_eng=("vector" if dt % 2 == 0 else "scalar"))
            emit_transpose(b3[:, dt * 128:(dt + 1) * 128], b3[:, dt * 128:(dt + 1) * 128],
                           128, evac_eng=("scalar" if dt % 2 == 0 else "vector"))
        k.wait("sync", s_v, s_v.n)
        k.wait("sync", s_a, s_a.n)
        k.dma("sync", w1ts[c], b1, s_wpw[par])
        k.dma("sync", w3ts[c], b3, s_wpw[par])
        wprep["written"][c] = s_wpw[par].n

    proj_bank = {"free": [(s_v, 0), (s_v, 0)]}

    def emit_proj_block(blk, is_own):
        t0 = blk * 512
        sl = slice(t0, t0 + 512)
        mts = (("qk", wqk), ("bd", wbd), ("bb2", wbb), ("v", wv))
        for mi, (mname, wsb) in enumerate(mts):
            par = mi % 2
            bank = pA0 if par == 0 else pA1
            sem, val = proj_bank["free"][par]
            k.wait("tensor", sem, val)
            if blk == 0 and mi == 0:
                k.wait("tensor", s_ld, ld_small)
            for dt in range(NDT):
                k.op("tensor", lambda t, w=wsb, dt=dt, b=bank: t.matmul(
                    b[:], w[:, dt * 128:(dt + 1) * 128],
                    xnT[:, dt * 512:(dt + 1) * 512],
                    start=(dt == 0), stop=(dt == NDT - 1)),
                    inc=(s_t if dt == NDT - 1 else None))
            mm_done = s_t.n
            if mname == "qk":
                k.wait("vector", s_t, mm_done)
                k.op("vector", lambda v, b=bank, sl=sl: v.tensor_copy(qkf[:, sl], b[:]),
                     inc=s_v)
                proj_bank["free"][par] = (s_v, s_v.n)
            elif mname == "v":
                k.wait("vector", s_t, mm_done)
                k.op("vector", lambda v, b=bank, sl=sl: v.tensor_copy(vfm[:, sl], b[:]),
                     inc=s_v)
                proj_bank["free"][par] = (s_v, s_v.n)
            elif mname == "bd":
                k.wait("scalar", s_t, mm_done)
                k.wait("scalar", s_v, bias_ready)
                k.wait("scalar", s_a, s_a.n)
                k.op("scalar", lambda s, b=bank: s.activation(
                    escr[0:64, 0:512], b[0:64, :], AF.Exp, bias=bias1[0:64, :], scale=-1.0),
                    inc=s_a)
                k.wait("scalar", s_a, s_a.n)
                k.op("scalar", lambda s, sl=sl: s.activation(
                    nlb2[0:64, sl], escr[0:64, 0:512], AF.Ln, bias=1.0), inc=s_a)
                k.wait("scalar", s_a, s_a.n)
                k.op("scalar", lambda s, b=bank: s.activation(
                    escr[64:128, 0:512], b[64:128, :], AF.Exp, bias=bias1[64:128, :],
                    scale=1.0), inc=s_a)
                k.wait("scalar", s_a, s_a.n)
                k.op("scalar", lambda s, sl=sl: s.activation(
                    dd[64:128, sl], escr[64:128, 0:512], AF.Ln, bias=1.0), inc=s_a)
                proj_bank["free"][par] = (s_a, s_a.n)
            else:
                k.wait("scalar", s_t, mm_done)
                k.wait("scalar", s_a, s_a.n)
                k.op("scalar", lambda s, b=bank: s.activation(
                    escr[64:128, 0:512], b[64:128, :], AF.Exp, bias=bias2[64:128, :],
                    scale=-1.0), inc=s_a)
                k.wait("scalar", s_a, s_a.n)
                k.op("scalar", lambda s, sl=sl: s.activation(
                    nlb2[64:128, sl], escr[64:128, 0:512], AF.Ln, bias=1.0), inc=s_a)
                proj_bank["free"][par] = (s_a, s_a.n)

    pair_info = {}
    mt_bank = {"free": [(s_v, 0), (s_v, 0)]}
    oT_free = {"m": (s_v, 0)}
    pP_free = {"m": (s_v, 0)}
    chain = {"ready": 0, "exps_done": 0}

    def emit_pair_indep(p, is_own):
        cols = slice(p * 128, (p + 1) * 128)
        ring = p % 2
        rsl = slice(ring * 128, (ring + 1) * 128)
        tm_ev = emit_transpose(kpf[:, cols], kptm[:, cols], 128)
        kpf0_ev = None
        if is_own:
            k.wait("tensor", tm_ev[0], tm_ev[1])
            kpf0_ev = emit_transpose(kptm[:, p * 128 + 64:p * 128 + 128], kpf0[0:64, cols], 64)
        vtm_ev = emit_transpose(vfm[:, cols], vtm[:, cols], 128)
        mt_ev = None
        if is_own:
            par = p % 2
            bank = pM0 if par == 0 else pM1
            sem, val = mt_bank["free"][par]
            k.wait("tensor", sem, val)
            k.wait("tensor", kpf0_ev[0], kpf0_ev[1])
            k.op("tensor", lambda t, b=bank: t.matmul(b[:, 0:128], kpf0[0:64, cols],
                                                      qp[0:64, cols], start=True, stop=True),
                 inc=s_t)
            mm_done = s_t.n
            k.wait("vector", s_t, mm_done)
            k.wait("vector", s_cst, consts_ld)
            k.op("vector", lambda v, b=bank: v.tensor_tensor(mtm[:, rsl], b[:, 0:128],
                                                             msk[:], OP.mult), inc=s_v)
            mt_bank["free"][par] = (s_v, s_v.n)
            mt_ev = s_v.n
        pair_info[p] = dict(mt_ev=mt_ev, tm_ev=tm_ev, vtm_ev=vtm_ev)

    def emit_pair_dep(p, is_own):
        cols = slice(p * 128, (p + 1) * 128)
        ring = p % 2
        info = pair_info[p]
        vsem, vval = info["vtm_ev"]
        tsem, tval = info["tm_ev"]
        if is_own:
            sem, val = oT_free["m"]
            k.wait("tensor", sem, val)
            k.wait("tensor", vsem, vval)
            k.wait("tensor", s_v, info["mt_ev"])
            k.op("tensor", lambda t: t.matmul(pOT[:, 0:128], vtm[:, cols],
                                              mtm[:, ring * 128:ring * 128 + 128],
                                              start=True, stop=False))
        for half in range(2):
            hsl = slice(p * 128 + half * 64, p * 128 + (half + 1) * 64)
            r0, r1 = half * 64, (half + 1) * 64
            if is_own:
                k.wait("tensor", s_st, chain["ready"])
                k.op("tensor", lambda t, hsl=hsl, half=half: t.matmul(
                    pOT[:, half * 64:half * 64 + 64], Sb[:], qp[0:64, hsl],
                    start=False, stop=(half == 1)), inc=(s_t if half == 1 else None))
            sem, val = pP_free["m"]
            k.wait("tensor", sem, val)
            k.wait("tensor", vsem, vval)
            k.wait("tensor", tsem, tval)
            if not is_own:
                k.wait("tensor", s_st, chain["ready"])
            k.op("tensor", lambda t, r0=r0, r1=r1: t.matmul(
                pP[0:64, 0:128], kptm[r0:r1, p * 128 + 64:p * 128 + 128],
                vtm[r0:r1, cols], start=True, stop=True), inc=s_t)
            pmm = s_t.n
            k.wait("vector", s_t, pmm)
            k.wait("vector", s_a, chain["exps_done"])
            k.op("vector", lambda v: v.tensor_tensor(Stmp[:], Sf[:], pP[0:64, 0:128], OP.add),
                 inc=s_v)
            pP_free["m"] = (s_v, s_v.n)
            k.wait("vector", s_v, s_v.n)
            k.op("vector", lambda v, p=p, half=half: v.tensor_scalar(
                Sf[:], Stmp[:], acb[:, 2 * p + half:2 * p + half + 1], None, OP.mult),
                 inc=s_v)
            k.wait("vector", s_v, s_v.n)
            k.op("vector", lambda v: v.tensor_copy(Sb[:], Sf[:]), inc=s_st)
            chain["ready"] = s_st.n
        if is_own:
            ot_done = s_t.n
            k.wait("vector", s_t, ot_done)
            k.op("vector", lambda v: v.tensor_copy(oT[:, cols], pOT[:, 0:128]), inc=s_v)
            oT_free["m"] = (s_v, s_v.n)

    xapply = {}
    xnb_free = [(s_v, 0), (s_v, 0)]

    def emit_segment(x_src, is_own):
        for tau in range(NT):
            par = tau % 2
            if tau >= 2:
                k.wait("scalar", s_a, xapply[tau - 2])
            k.dma("scalar", xt[:, par * D:(par + 1) * D],
                  x_src[tau * 128:(tau + 1) * 128, :], s_xt[par])
            ld = s_xt[par].n
            k.wait("vector", s_xt[par], ld)
            k.wait("scalar", s_xt[par], ld)
            a_done = emit_ln_stats_apply(xt[:, par * D:(par + 1) * D],
                                         xnb[:, par * D:(par + 1) * D],
                                         dst_free=xnb_free[par])
            xapply[tau] = a_done
            k.wait("tensor", s_a, a_done)
            bi = tau % 4
            for dt in range(NDT):
                emit_transpose(xnb[:, par * D + dt * 128:par * D + (dt + 1) * 128],
                               xnT[:, dt * 512 + bi * 128:dt * 512 + bi * 128 + 128], 128,
                               evac_eng=("vector" if dt % 2 == 0 else "scalar"))
            xnb_free[par] = (s_t, s_t.n)
            if bi == 3:
                k.wait("tensor", s_v, s_v.n)
                k.wait("tensor", s_a, s_a.n)
                emit_proj_block(tau // 4, is_own)
        k.wait("vector", s_a, s_a.n)
        k.wait("vector", s_v, s_v.n)
        for ch in range(2 * NP):
            k.op("vector", lambda v, o=ch * 64: v.tensor_tensor_scan(
                scum[:, o:o + 64], nlb2[:, o:o + 64], nlb2[:, o:o + 64],
                0.0, OP.add, OP.bypass), inc=(s_v if ch == 2 * NP - 1 else None))
        scan_done = s_v.n
        k.wait("scalar", s_v, scan_done)
        if is_own:
            k.op("scalar", lambda s: s.activation(aa[0:64, :], scum[0:64, :], AF.Exp,
                                                  scale=-1.0))
        k.op("scalar", lambda s: s.activation(aa[64:128, :], scum[64:128, :], AF.Exp,
                                              scale=1.0))
        scum_last = scum.rearrange("p (ch c) -> p ch c", c=64)
        k.op("scalar", lambda s: s.activation(acb[:, 0:32], scum_last[0:64, :, 63], AF.Exp,
                                              scale=-1.0), inc=s_a)
        exps_done = s_a.n
        chain["exps_done"] = exps_done
        k.wait("vector", s_a, exps_done)
        if is_own:
            k.op("vector", lambda v: v.tensor_tensor(qp[0:64, :], qkf[0:64, :],
                                                     aa[0:64, :], OP.mult))
        k.op("vector", lambda v: v.tensor_tensor(kpw[64:128, :], qkf[64:128, :],
                                                 dd[64:128, :], OP.mult), inc=s_v)
        k.wait("vector", s_v, s_v.n)
        k.op("vector", lambda v: v.tensor_tensor(kpf[64:128, :], kpw[64:128, :],
                                                 aa[64:128, :], OP.mult), inc=s_v)
        elt_done = s_v.n
        k.wait("tensor", s_v, elt_done)
        if is_own:
            wprep["proj_done"] = s_t.n
        for p in range(NP + 1):
            if p < NP:
                emit_pair_indep(p, is_own)
            if p > 0:
                emit_pair_dep(p - 1, is_own)
            if is_own:
                emit_wprep_block()

    k.wait("vector", s_v, memset_done)
    emit_segment(xp_d, is_own=False)
    emit_segment(xo_d, is_own=True)

    x1_written = []
    att_ready = 0
    a4_xt = {}
    for tau in range(NT):
        par = tau % 2
        for nb in range(4):
            bpar = nb % 2
            bank = pA0 if bpar == 0 else pA1
            sem, val = proj_bank["free"][bpar]
            k.wait("tensor", sem, val)
            if tau == 0 and nb == 0:
                sem2, val2 = oT_free["m"]
                k.wait("tensor", sem2, val2)
            k.op("tensor", lambda t, nb=nb, tau=tau, b=bank: t.matmul(
                b[:], oT[:, tau * 128:(tau + 1) * 128], wo[:, nb * 512:(nb + 1) * 512],
                start=True, stop=True), inc=s_t)
            mm_done = s_t.n
            k.wait("vector", s_t, mm_done)
            if att_ready:
                k.wait("vector", s_a, att_ready)
            else:
                k.wait("vector", s_a, chain["exps_done"])
            k.op("vector", lambda v, nb=nb, b=bank: v.tensor_copy(
                att[:, nb * 512:(nb + 1) * 512], b[:]), inc=s_v)
            proj_bank["free"][bpar] = (s_v, s_v.n)
        att_done = s_v.n
        if tau >= 2:
            k.wait("scalar", s_v, a4_xt[tau - 2])
        else:
            k.wait("scalar", s_a, xapply[NT - 2 + tau])
        k.dma("scalar", xt[:, par * D:(par + 1) * D], xo_d[tau * 128:(tau + 1) * 128, :],
              s_xt[par])
        xld = s_xt[par].n
        k.wait("vector", s_v, att_done)
        k.wait("scalar", s_v, att_done)
        if x1_written:
            k.wait("scalar", s_x1w, x1_written[-1])
        a_done = emit_ln_stats_apply(att[:, 0:D], x1t[:, 0:D])
        att_ready = a_done
        k.wait("vector", s_a, a_done)
        k.wait("vector", s_xt[par], xld)
        k.op("vector", lambda v, par=par: v.tensor_tensor(
            x1t[:, 0:D], x1t[:, 0:D], xt[:, par * D:(par + 1) * D], OP.add), inc=s_v)
        x1_done = s_v.n
        a4_xt[tau] = x1_done

        k.wait("scalar", s_v, x1_done)
        k.dma("scalar", x1s[tau * 128:(tau + 1) * 128, :], x1t[:, 0:D], s_x1w)
        x1_written.append(s_x1w.n)
        k.wait("vector", s_v, x1_done)
        k.wait("scalar", s_v, x1_done)
        a2_done = emit_ln_stats_apply(x1t[:, 0:D], xnb[:, par * D:(par + 1) * D],
                                      dst_free=xnb_free[par])
        k.wait("tensor", s_a, a2_done)
        for dt in range(NDT):
            emit_transpose(xnb[:, par * D + dt * 128:par * D + (dt + 1) * 128],
                           xn2T[:, dt * TH + tau * 128:dt * TH + tau * 128 + 128], 128,
                           evac_eng=("vector" if dt % 2 == 0 else "scalar"))
        xnb_free[par] = (s_t, s_t.n)
        emit_wprep_block()
        emit_wprep_block()
        emit_wprep_block()

    x1_all_written = s_x1w.n
    xn2T_v, xn2T_a = s_v.n, s_a.n

    while wprep["next"] < NC:
        emit_wprep_block()
    k.barrier()
    al.free_from(amark)
    bmark = al.mark()
    w1t = al.sb("w1t_b", [128, 2 * D], BF16)
    w3t = al.sb("w3t_b", [128, 2 * D], BF16)
    hid = al.sb("hid_b", [128, 2 * 512], BF16)
    zbr = al.sb("zbr_b", [128, 2 * TH], BF16)

    wtrans_done = []
    zb_written = []
    zb_count = [0, 0]
    bst = {"ug": (s_v, 0), "ul": (s_v, 0), "zb": (s_v, 0),
           "hid": [(s_v, 0), (s_v, 0)], "pend_bp": []}

    def emit_bp(job):
        hsl, tsl, par, mul_done, hpar = job
        sem, val = bst["zb"]
        k.wait("tensor", sem, val)
        k.wait("tensor", s_v, mul_done)
        k.op("tensor", lambda t, hsl=hsl: t.matmul(pM1[:], hb1[:], hid[:, hsl],
                                                   start=True, stop=True), inc=s_t)
        zb_mm = s_t.n
        bst["hid"][hpar] = (s_t, zb_mm)
        k.wait("vector", s_t, zb_mm)
        k.op("vector", lambda v, tsl=tsl, par=par: v.tensor_copy(
            zbr[:, par * TH + tsl.start:par * TH + tsl.stop], pM1[:]), inc=s_v)
        bst["zb"] = (s_v, s_v.n)
        zb_count[par] += 1
        if zb_count[par] % 4 == 0:
            cc = (zb_count[par] - 1) // 4 * 2 + par
            k.wait("sync", s_v, s_v.n)
            k.dma("sync", zbs[:, cc, :], zbr[:, par * TH:(par + 1) * TH], s_zbw[par])
            zb_written[cc] = s_zbw[par].n

    for c in range(NC):
        par = c % 2
        dsl = slice(par * D, (par + 1) * D)
        if c == 0:
            k.wait("gpsimd", s_wpw[0], s_wpw[0].n)
            k.wait("gpsimd", s_wpw[1], s_wpw[1].n)
        if c >= 2:
            sem, val = wtrans_done[c - 2]
            k.wait("gpsimd", sem, val)
        k.dma("gpsimd", w1t[:, dsl], w1ts[c], s_wl[par])
        k.dma("gpsimd", w3t[:, dsl], w3ts[c], s_wl[par])
        wl = s_wl[par].n
        k.wait("tensor", s_wl[par], wl)
        if c == 0:
            k.wait("tensor", s_v, xn2T_v)
            k.wait("tensor", s_a, xn2T_a)
        if c >= 2:
            assert zb_written[c - 2] is not None
            k.wait("vector", s_zbw[par], zb_written[c - 2])

        for ts in range(4):
            tsl = slice(ts * 512, (ts + 1) * 512)
            hpar = ts % 2
            hsl = slice(hpar * 512, (hpar + 1) * 512)
            sem, val = bst["ug"]
            k.wait("tensor", sem, val)
            for dt in range(NDT):
                k.op("tensor", lambda t, dt=dt, ts=ts, par=par: t.matmul(
                    pA0[:], w1t[:, par * D + dt * 128:par * D + (dt + 1) * 128],
                    xn2T[:, dt * TH + ts * 512:dt * TH + ts * 512 + 512],
                    start=(dt == 0), stop=(dt == NDT - 1)),
                    inc=(s_t if dt == NDT - 1 else None))
            ug_done = s_t.n
            sem, val = bst["ul"]
            k.wait("tensor", sem, val)
            for dt in range(NDT):
                k.op("tensor", lambda t, dt=dt, ts=ts, par=par: t.matmul(
                    pA1[:], w3t[:, par * D + dt * 128:par * D + (dt + 1) * 128],
                    xn2T[:, dt * TH + ts * 512:dt * TH + ts * 512 + 512],
                    start=(dt == 0), stop=(dt == NDT - 1)),
                    inc=(s_t if dt == NDT - 1 else None))
            ul_done = s_t.n
            sem, val = bst["hid"][hpar]
            k.wait("scalar", sem, val)
            k.wait("scalar", s_t, ug_done)
            k.op("scalar", lambda s, hsl=hsl: s.activation(hid[:, hsl], pA0[:], AF.Silu),
                 inc=s_a)
            silu_done = s_a.n
            bst["ug"] = (s_a, silu_done)
            k.wait("vector", s_a, silu_done)
            k.wait("vector", s_t, ul_done)
            k.op("vector", lambda v, hsl=hsl: v.tensor_tensor(hid[:, hsl], hid[:, hsl],
                                                              pA1[:], OP.mult), inc=s_v)
            mul_done = s_v.n
            bst["ul"] = (s_v, mul_done)
            bst["pend_bp"].append((hsl, tsl, par, mul_done, hpar))
            if len(bst["pend_bp"]) > 1:
                emit_bp(bst["pend_bp"].pop(0))
        wtrans_done.append((s_t, s_t.n))
        zb_written.append(None)
        if c == NC - 1:
            while bst["pend_bp"]:
                emit_bp(bst["pend_bp"].pop(0))

    k.barrier()
    al.free_from(bmark)
    al.free_from(xn2T_mark)
    al.free_from(aw_mark)
    cmark = al.mark()
    ZCH = 4
    GR = 4
    gbuf = al.sb("g_b", [128, GR * TH], BF16)
    zfr = al.sb("zfr_b", [128, 2 * TH], BF16)
    zfl = al.sb("zfl_b", [128, 2 * ZCH * 512], BF16)
    w2t = al.sb("w2t_b", [128, 2 * 64 * 512], BF16)
    otb = al.sb("otb_b", [128, 4 * 512], BF16)

    zb_view = zbs.rearrange("(j q) c t -> q j c t", j=2)
    zfs_view = zfs.rearrange("g p t -> p g t")

    gfree = [(s_v, 0)] * GR
    zfr_free = [(s_v, 0)] * 2
    banks4 = (pA0, pA1, pM0, pOT)
    cbank = [(s_v, 0)] * 4
    k.wait("sync", s_zbw[0], s_zbw[0].n)
    k.wait("sync", s_zbw[1], s_zbw[1].n)
    for g1 in range(64):
        r = g1 % GR
        r2 = g1 % 2
        gb = slice(r * TH, (r + 1) * TH)
        zb_ = slice(r2 * TH, (r2 + 1) * TH)
        sem, val = gfree[r]
        k.wait("sync", sem, val)
        k.dma("sync", gbuf[:, gb], zb_view[g1], s_gg[r])
        gld = s_gg[r].n
        psem, pval = zfr_free[r2]
        k.wait("vector", psem, pval)
        k.wait("tensor", s_gg[r], gld)
        for i in range(4):
            bank = banks4[i]
            sem, val = cbank[i]
            k.wait("tensor", sem, val)
            k.op("tensor", lambda t, o=gb.start + i * 512, b=bank: t.matmul(
                b[:], hb2[:], gbuf[:, o:o + 512], start=True, stop=True), inc=s_t)
            cmms = s_t.n
            k.wait("vector", s_t, cmms)
            k.op("vector", lambda v, o=zb_.start + i * 512, b=bank: v.tensor_scalar(
                zfr[:, o:o + 512], b[:], RS, None, OP.mult), inc=s_v)
            cbank[i] = (s_v, s_v.n)
        gfree[r] = (s_t, s_t.n)
        ev = s_v.n
        k.wait("sync", s_v, ev)
        k.dma("sync", zfs[g1], zfr[:, zb_], s_zfw[g1 % 4])
        zfr_free[r2] = (s_zfw[g1 % 4], s_zfw[g1 % 4].n)

    banksC = (pM0, pOT, pP, pM1)
    w2t_ld = [None] * 4
    grp_mm_done = [None] * 4

    def emit_w2t_loads(grp):
        gpar = grp % 2
        base = gpar * 64 * 512
        if grp >= 2:
            sem, val = grp_mm_done[grp - 2]
            k.wait("scalar", sem, val)
        if grp == 0:
            k.wait("scalar", s_w2c, w2cast_done)
        for a1 in range(64):
            k.dmat("scalar", w2t[:, base + a1 * 512:base + (a1 + 1) * 512],
                   w2s[grp * 512:(grp + 1) * 512, a1 * 128:(a1 + 1) * 128], s_w2t[gpar])
        w2t_ld[grp] = s_w2t[gpar].n

    emit_w2t_loads(0)
    emit_w2t_loads(1)

    zfl_free = [(s_v, 0)] * 2
    cmb_free = [cbank[2], cbank[3], (s_v, 0), (s_v, 0)]
    otb_gate = 0
    for r in range(4):
        k.wait("sync", s_zfw[r], s_zfw[r].n)
    ffs_dT_view = ffs_dT.rearrange("(a p) t -> p a t", p=128)
    for grp in range(4):
        gpar = grp % 2
        base = gpar * 64 * 512
        k.wait("tensor", s_w2t[grp % 2], w2t_ld[grp])
        for ts in range(4):
            for bi in range(4):
                sem, val = cmb_free[bi]
                k.wait("tensor", sem, val)
            for zc in range(64 // ZCH):
                r = zc % 2
                a1a = zc * ZCH
                base_r = r * ZCH * 512
                sem, val = zfl_free[r]
                k.wait("sync", sem, val)
                k.dma("sync", zfl[:, base_r:base_r + ZCH * 512].rearrange(
                    "p (q t) -> p q t", q=ZCH),
                    zfs_view[:, a1a:a1a + ZCH, ts * 512:(ts + 1) * 512], s_zfl[r])
                zld = s_zfl[r].n
                k.wait("tensor", s_zfl[r], zld)
                for i in range(ZCH):
                    a1 = a1a + i
                    for dt in range(4):
                        k.op("tensor", lambda t, a1=a1, dt=dt, o=base_r + i * 512,
                             base=base: t.matmul(
                            banksC[dt][:],
                            w2t[:, base + a1 * 512 + dt * 128:base + a1 * 512 + (dt + 1) * 128],
                            zfl[:, o:o + 512], start=(a1 == 0), stop=(a1 == 63)),
                            inc=(s_t if dt == 3 else None))
                zfl_free[r] = (s_t, s_t.n)
            acc_done = s_t.n
            k.wait("vector", s_t, acc_done)
            if otb_gate:
                k.wait("vector", s_ffw, otb_gate)
            for dt in range(4):
                k.op("vector", lambda v, dt=dt: v.tensor_copy(
                    otb[:, dt * 512:(dt + 1) * 512], banksC[dt][:]), inc=s_v)
                cmb_free[dt] = (s_v, s_v.n)
            evd = s_v.n
            k.wait("sync", s_v, evd)
            k.dma("sync", ffs_dT_view[:, grp * 4:(grp + 1) * 4, ts * 512:(ts + 1) * 512],
                  otb[:].rearrange("p (q t) -> p q t", q=4), s_ffw)
            otb_gate = s_ffw.n
        grp_mm_done[grp] = (s_t, s_t.n)
        if grp + 2 < 4:
            emit_w2t_loads(grp + 2)
    ffs_all = s_ffw.n

    k.barrier()
    al.free_from(cmark)
    fft = al.sb("fft_b", [128, 2 * D], BF16)
    out_res = []
    out_dma = []
    for tau in range(NT):
        par = tau % 2
        dsl = slice(par * D, (par + 1) * D)
        if tau == 0:
            k.wait("scalar", s_ffw, ffs_all)
            k.wait("scalar", s_x1w, x1_all_written)
        if tau >= 2:
            k.wait("scalar", s_v, out_res[tau - 2])
        k.dmat("scalar", fft[:, dsl], ffs_dT[:, tau * 128:(tau + 1) * 128], s_ffl[par])
        k.dma("scalar", xt[:, dsl], x1s[tau * 128:(tau + 1) * 128, :], s_ffl[par])
        ld = s_ffl[par].n
        if tau >= 1:
            k.wait("scalar", s_out, out_dma[tau - 1])
        k.wait("vector", s_ffl[par], ld)
        k.wait("scalar", s_ffl[par], ld)
        a_done = emit_ln_stats_apply(fft[:, dsl], x1t[:, 0:D])
        k.wait("vector", s_a, a_done)
        k.op("vector", lambda v, dsl=dsl: v.tensor_tensor(x1t[:, 0:D], x1t[:, 0:D],
                                                          xt[:, dsl], OP.add), inc=s_v)
        out_res.append(s_v.n)
        k.wait("sync", s_v, s_v.n)
        k.dma("sync", out_d[tau * 128:(tau + 1) * 128, :], x1t[:, 0:D], s_out)
        out_dma.append(s_out.n)
    k.wait("sync", s_out, s_out.n)

    with nc.Block() as block:
        k.build(block)
    k.close_sems()
    for ctx in reversed(psum_ctxs):
        ctx.__exit__(None, None, None)
    al.free_from(0)
    return nc


def _consts():
    idx = np.arange(64)
    pc = np.zeros((64, 64), dtype=np.int64)
    for i in range(64):
        pc[i] = [bin(i & j).count("1") for j in range(64)]
    H64 = np.where(pc % 2 == 0, 1.0, -1.0).astype(np.float32)
    H2 = np.array([[1.0, 1.0], [1.0, -1.0]], dtype=np.float32)
    hb1 = np.einsum("ab,cd->acbd", H2, H64).reshape(128, 128)
    hb2 = np.zeros((128, 128), dtype=np.float32)
    hb2[0:64, 0:64] = H64
    hb2[64:128, 64:128] = H64
    tri = np.triu(np.ones((64, 64), dtype=np.float32))
    mask = np.zeros((128, 128), dtype=np.float32)
    mask[0:64, 0:64] = tri
    mask[64:128, 64:128] = tri
    bf = ml_dtypes.bfloat16
    return (hb1.astype(bf), hb2.astype(bf), mask.astype(bf), np.eye(128, dtype=bf))


def kernel(L0_attn_x, L0_attn_Wq, L0_attn_Wk, L0_attn_Wv, L0_attn_Wo,
           L0_attn_Wb, L0_attn_bb, L0_attn_Wd, L0_attn_bd,
           L0_ffn_W1, L0_ffn_W2, L0_ffn_W3):
    x = np.ascontiguousarray(np.asarray(L0_attn_x, dtype=np.float32))
    wqk = np.ascontiguousarray(np.concatenate([L0_attn_Wq, L0_attn_Wk], axis=1).astype(np.float32))
    wbd = np.ascontiguousarray(np.concatenate([L0_attn_Wb, L0_attn_Wd], axis=1).astype(np.float32))
    wbb = np.ascontiguousarray(np.concatenate([L0_attn_Wb, L0_attn_Wb], axis=1).astype(np.float32))
    wv = np.ascontiguousarray(np.asarray(L0_attn_Wv, dtype=np.float32))
    wo = np.ascontiguousarray(np.asarray(L0_attn_Wo, dtype=np.float32))
    bb = np.ascontiguousarray(np.asarray(L0_attn_bb, dtype=np.float32).reshape(64, 1))
    bd = np.ascontiguousarray(np.asarray(L0_attn_bd, dtype=np.float32).reshape(64, 1))
    w1 = np.ascontiguousarray(np.asarray(L0_ffn_W1, dtype=np.float32))
    w2 = np.ascontiguousarray(np.asarray(L0_ffn_W2, dtype=np.float32))
    w3 = np.ascontiguousarray(np.asarray(L0_ffn_W3, dtype=np.float32))
    hb1, hb2, mask, idn = _consts()
    zeros = np.zeros((TH, D), dtype=np.float32)

    nc = build_nc()
    in_maps = []
    for core in range(8):
        b, half = core // 2, core % 2
        xo = np.ascontiguousarray(x[b, half * TH:(half + 1) * TH, :])
        xp = np.ascontiguousarray(x[b, 0:TH, :]) if half == 1 else zeros
        in_maps.append({
            "xo": xo, "xp": xp, "wqk": wqk, "wbd": wbd, "wbb": wbb, "wv": wv,
            "wo": wo, "bb": bb, "bd": bd, "w1": w1, "w2": w2, "w3": w3,
            "hb1": hb1, "hb2": hb2, "msk": mask, "idn": idn,
        })
    res = run_bass_kernel_spmd(nc, in_maps, core_ids=list(range(8)))
    out = np.empty((4, 4096, 2048), dtype=np.float32)
    for core in range(8):
        b, half = core // 2, core % 2
        out[b, half * TH:(half + 1) * TH, :] = res.results[core]["out"]
    return out


# version 08
# speedup vs baseline: 1.0152x; 1.0152x over previous
import numpy as np
import ml_dtypes
import concourse.bass as bass
import concourse.mybir as mybir
from concourse.bass_utils import run_bass_kernel_spmd
from concourse.alu_op_type import AluOpType as OP

F32 = mybir.dt.float32
BF16 = mybir.dt.bfloat16
AF = mybir.ActivationFunctionType
AX = mybir.AxisListType

TH = 2048
D = 2048
H = 8192
NT = TH // 128
NP = TH // 128
NC = 64
NDT = D // 128
RS = float(1.0 / np.sqrt(8192.0))

ENGINES = ("tensor", "vector", "scalar", "gpsimd", "sync")


class Sem:
    __slots__ = ("h", "n", "name")

    def __init__(self, h, name):
        self.h = h
        self.n = 0
        self.name = name


class K:
    def __init__(self, nc):
        self.nc = nc
        self.items = {e: [] for e in ENGINES}
        self.sem_ctxs = []
        self.sems = []

    def sem(self, name):
        ctx = self.nc.semaphore(name)
        h = ctx.__enter__()
        self.sem_ctxs.append(ctx)
        s = Sem(h, name)
        self.sems.append(s)
        return s

    def wait(self, eng, sem, val):
        assert val <= sem.n, f"wait on {sem.name} for {val} but only {sem.n} emitted"
        if val > 0:
            self.items[eng].append(("w", sem, val))

    def op(self, eng, fn, inc=None, amt=1):
        self.items[eng].append(("o", fn, inc, amt))
        if inc is not None:
            inc.n += amt
            return inc.n
        return None

    def dma(self, eng, dst, src, sem, **kw):
        return self.op(eng, lambda e: e.dma_start(dst, src, **kw), inc=sem, amt=16)

    def dmat(self, eng, dst, src, sem):
        return self.op(eng, lambda e: e.dma_start_transpose(dst, src), inc=sem, amt=16)

    def barrier(self):
        snap = [(s, s.n) for s in self.sems if s.n > 0]
        for e in ENGINES:
            for s, v in snap:
                self.items[e].append(("w", s, v))

    def build(self, block):
        for eng in ENGINES:
            dec = getattr(block, eng)
            items = self.items[eng]

            def make(items=items):
                def body(h):
                    for it in items:
                        if it[0] == "w":
                            h.wait_ge(it[1].h, it[2])
                        else:
                            inst = it[1](h)
                            if it[2] is not None:
                                inst.then_inc(it[2].h, it[3])
                return body
            dec(make())

    def close_sems(self):
        for ctx in reversed(self.sem_ctxs):
            ctx.__exit__(None, None, None)


class Alloc:
    def __init__(self, nc):
        self.nc = nc
        self.stack = []

    def sb(self, name, shape, dtype):
        ctx = self.nc.sbuf_tensor(name, shape, dtype)
        h = ctx.__enter__()
        self.stack.append((name, ctx))
        return h

    def free_from(self, mark):
        while len(self.stack) > mark:
            name, ctx = self.stack.pop()
            ctx.__exit__(None, None, None)

    def mark(self):
        return len(self.stack)


def build_nc():
    nc = bass.Bass("TRN2", target_bir_lowering=False)

    xo_d = nc.dram_tensor("xo", [TH, D], F32, kind="ExternalInput").ap()
    xp_d = nc.dram_tensor("xp", [TH, D], F32, kind="ExternalInput").ap()
    wqk_d = nc.dram_tensor("wqk", [D, 128], F32, kind="ExternalInput").ap()
    wbd_d = nc.dram_tensor("wbd", [D, 128], F32, kind="ExternalInput").ap()
    wbb_d = nc.dram_tensor("wbb", [D, 128], F32, kind="ExternalInput").ap()
    wv_d = nc.dram_tensor("wv", [D, 128], F32, kind="ExternalInput").ap()
    wo_d = nc.dram_tensor("wo", [128, D], F32, kind="ExternalInput").ap()
    bb_d = nc.dram_tensor("bb", [64, 1], F32, kind="ExternalInput").ap()
    bd_d = nc.dram_tensor("bd", [64, 1], F32, kind="ExternalInput").ap()
    w1_d = nc.dram_tensor("w1", [H, D], F32, kind="ExternalInput").ap()
    w2_d = nc.dram_tensor("w2", [D, H], F32, kind="ExternalInput").ap()
    w3_d = nc.dram_tensor("w3", [H, D], F32, kind="ExternalInput").ap()
    hb1_d = nc.dram_tensor("hb1", [128, 128], BF16, kind="ExternalInput").ap()
    hb2_d = nc.dram_tensor("hb2", [128, 128], BF16, kind="ExternalInput").ap()
    msk_d = nc.dram_tensor("msk", [128, 128], BF16, kind="ExternalInput").ap()
    idn_d = nc.dram_tensor("idn", [128, 128], BF16, kind="ExternalInput").ap()

    out_d = nc.dram_tensor("out", [TH, D], F32, kind="ExternalOutput").ap()

    w2s = nc.dram_tensor("w2s", [D, H], BF16, kind="Internal").ap()
    x1s = nc.dram_tensor("x1s", [TH, D], F32, kind="Internal").ap()
    zbs = nc.dram_tensor("zbs", [128, NC, TH], BF16, kind="Internal").ap()
    zfs = nc.dram_tensor("zfs", [64, 128, TH], BF16, kind="Internal").ap()
    ffs_dT = nc.dram_tensor("ffs", [D, TH], BF16, kind="Internal").ap()

    al = Alloc(nc)

    idn = al.sb("idn_s", [128, 128], BF16)
    hb1 = al.sb("hb1_s", [128, 128], BF16)
    hb2 = al.sb("hb2_s", [128, 128], BF16)
    msk = al.sb("msk_s", [128, 128], BF16)
    stat = al.sb("stat_s", [128, 16], F32)
    sqs = al.sb("sqs_s", [128, D], BF16)
    xt = al.sb("xt_s", [128, 2 * D], F32)
    xnb = al.sb("xnb_s", [128, 2 * D], BF16)
    x1t = al.sb("x1t_s", [128, D], F32)
    epsb = al.sb("epsb_s", [128, 1], F32)

    aw_mark = al.mark()
    wqk = al.sb("wqk_s", [128, NDT * 128], BF16)
    wbd = al.sb("wbd_s", [128, NDT * 128], BF16)
    wbb = al.sb("wbb_s", [128, NDT * 128], BF16)
    wv = al.sb("wv_s", [128, NDT * 128], BF16)
    wo = al.sb("wo_s", [128, D], BF16)
    bias1 = al.sb("bias1_s", [128, 1], F32)
    bias2 = al.sb("bias2_s", [128, 1], F32)

    xn2T_mark = al.mark()
    xn2T = al.sb("xn2T_b", [128, NDT * TH], BF16)

    psum_ctxs = []

    def ps(name, shape, dtype):
        ctx = nc.psum_tensor(name, shape, dtype)
        h = ctx.__enter__()
        psum_ctxs.append(ctx)
        return h

    pA0 = ps("pA0", [128, 512], F32)
    pA1 = ps("pA1", [128, 512], F32)
    pT0 = ps("pT0", [128, 1024], BF16)
    pT1 = ps("pT1", [128, 1024], BF16)
    pM0 = ps("pM0", [128, 512], F32)
    pOT = ps("pOT", [128, 512], F32)
    pP = ps("pP", [128, 512], F32)
    pM1 = ps("pM1", [128, 512], F32)

    k = K(nc)
    s_cst = k.sem("s_cst")
    s_ld = k.sem("s_ld")
    s_w2c = k.sem("s_w2c")
    s_wl = [k.sem("s_wl0"), k.sem("s_wl1")]
    s_xt = [k.sem("s_xt0"), k.sem("s_xt1")]
    s_x1w = k.sem("s_x1w")
    s_zbw = [k.sem("s_zbw0"), k.sem("s_zbw1")]
    s_gg = [k.sem(f"s_gg{i}") for i in range(4)]
    s_zfw = [k.sem(f"s_zfw{i}") for i in range(4)]
    s_zfl = [k.sem(f"s_zfl{i}") for i in range(4)]
    s_w2t = [k.sem("s_w2t0"), k.sem("s_w2t1")]
    s_ffw = k.sem("s_ffw")
    s_ffl = [k.sem("s_ffl0"), k.sem("s_ffl1")]
    s_t = k.sem("s_t")
    s_v = k.sem("s_v")
    s_a = k.sem("s_a")
    s_st = k.sem("s_st")
    s_out = k.sem("s_out")

    k.dma("sync", idn[:], idn_d, s_cst)
    k.dma("sync", hb1[:], hb1_d, s_cst)
    k.dma("sync", hb2[:], hb2_d, s_cst)
    k.dma("sync", msk[:], msk_d, s_cst)
    consts_ld = s_cst.n
    for dst, src in ((wqk, wqk_d), (wbd, wbd_d), (wbb, wbb_d), (wv, wv_d)):
        k.dma("gpsimd", dst[:], src.rearrange("(dt p) f -> p dt f", p=128), s_ld)
    k.dma("gpsimd", wo[:], wo_d, s_ld)
    k.dma("gpsimd", bias1[0:64, :], bb_d, s_ld)
    k.dma("gpsimd", bias1[64:128, :], bd_d, s_ld)
    k.dma("gpsimd", bias2[64:128, :], bb_d, s_ld)
    ld_small = s_ld.n
    for ch in range(32):
        k.dma("gpsimd", w2s[ch * 64:(ch + 1) * 64, :], w2_d[ch * 64:(ch + 1) * 64, :], s_w2c)
    w2cast_done = s_w2c.n

    k.wait("vector", s_ld, ld_small)
    k.op("vector", lambda v: v.tensor_scalar(bias1[0:64, :], bias1[0:64, :], -1.0, None, OP.mult))
    k.op("vector", lambda v: v.tensor_scalar(bias2[64:128, :], bias2[64:128, :], -1.0, None,
                                             OP.mult), inc=s_v)
    k.op("vector", lambda v: v.memset(epsb[:], 1e-5), inc=s_v)
    bias_ready = s_v.n
    k.wait("tensor", s_cst, consts_ld)

    amark = al.mark()
    xnT = al.sb("xnT_b", [128, NDT * 512], BF16)
    qkf = al.sb("qkf_b", [128, TH], BF16)
    nlb2 = al.sb("nlb2_b", [128, TH], F32)
    dd = al.sb("dd_b", [128, TH], BF16)
    vfm = al.sb("vfm_b", [128, TH], BF16)
    escr = al.sb("escr_b", [128, 512], F32)
    scum = al.sb("scum_b", [128, TH], F32)
    aa = al.sb("aa_b", [128, TH], BF16)
    qp = al.sb("qp_b", [128, TH], BF16)
    kpw = al.sb("kpw_b", [128, TH], BF16)
    kpf = al.sb("kpf_b", [128, TH], BF16)
    kptm = al.sb("kptm_b", [128, TH], BF16)
    kpf0 = al.sb("kpf0_b", [128, TH], BF16)
    vtm = al.sb("vtm_b", [128, TH], BF16)
    mtm = al.sb("mtm_b", [128, 256], BF16)
    acb = al.sb("acb_b", [64, 32], F32)
    Sf = al.sb("Sf_b", [64, 128], F32)
    Sb = al.sb("Sb_b", [64, 128], BF16)
    Stmp = al.sb("Stmp_b", [64, 128], F32)
    oT = al.sb("oT_b", [128, TH], BF16)
    att = scum

    k.op("vector", lambda v: v.memset(Sf[:], 0.0))
    k.op("vector", lambda v: v.memset(Sb[:], 0.0))
    k.op("vector", lambda v: v.memset(kpf[:], 0.0))
    k.op("vector", lambda v: v.memset(qp[:], 0.0))
    k.op("vector", lambda v: v.memset(kpf0[:], 0.0), inc=s_v)
    memset_done = s_v.n

    ln_state = {"slot": 0, "free": [(s_v, 0), (s_v, 0)]}

    def emit_ln_stats_apply(src_ap, dst_ap, dst_free=None):
        sl = ln_state["slot"]
        ln_state["slot"] ^= 1
        c0 = sl * 8

        def st_(i):
            return stat[:, c0 + i:c0 + i + 1]

        fsem, fval = ln_state["free"][sl]
        k.wait("vector", fsem, fval)
        k.op("vector", lambda v: v.reduce_sum(st_(0), src_ap, axis=AX.X), inc=s_v)
        rdone = s_v.n
        k.wait("scalar", fsem, fval)
        k.op("scalar", lambda s: s.activation(sqs[:, 0:D], src_ap, AF.Square,
                                              accum_out=st_(1)), inc=s_a)
        sq_done = s_a.n
        k.wait("vector", s_v, rdone)
        k.op("vector", lambda v: v.tensor_scalar(st_(2), st_(0), 1.0 / D, None, OP.mult),
             inc=s_v)
        k.wait("vector", s_a, sq_done)
        k.wait("vector", s_v, s_v.n)
        k.op("vector", lambda v: v.tensor_scalar(st_(3), st_(1), 1.0 / D, None, OP.mult))
        k.op("vector", lambda v: v.tensor_tensor(st_(4), st_(2), st_(2), OP.mult), inc=s_v)
        k.wait("vector", s_v, s_v.n)
        k.op("vector", lambda v: v.tensor_tensor(st_(4), st_(3), st_(4), OP.subtract),
             inc=s_v)
        var_done = s_v.n
        k.wait("scalar", s_v, var_done)
        k.op("scalar", lambda s: s.activation(st_(5), st_(4), AF.Ln, bias=epsb[:, 0:1]),
             inc=s_a)
        k.wait("scalar", s_a, s_a.n)
        k.op("scalar", lambda s: s.activation(st_(6), st_(5), AF.Exp, scale=-0.5), inc=s_a)
        rstd_done = s_a.n
        k.wait("vector", s_a, rstd_done)
        k.op("vector", lambda v: v.scalar_tensor_tensor(st_(7), st_(2), -1.0, st_(6),
                                                        OP.mult, OP.mult), inc=s_v)
        nmr_done = s_v.n
        k.wait("scalar", s_v, nmr_done)
        if dst_free is not None:
            k.wait("scalar", dst_free[0], dst_free[1])
        k.op("scalar", lambda s: s.activation(dst_ap, src_ap, AF.Identity,
                                              bias=st_(7), scale=st_(6)), inc=s_a)
        ln_state["free"][sl] = (s_a, s_a.n)
        return s_a.n

    tp_state = {"n": 0}
    tp_free = [(s_v, 0), (s_v, 0)]

    def emit_transpose(in_ap, out_ap, npart, evac_eng="vector"):
        par = tp_state["n"] % 2
        tp_state["n"] += 1
        bank = pT0 if par == 0 else pT1
        sem, val = tp_free[par]
        k.wait("tensor", sem, val)
        k.op("tensor", lambda t, b=bank, i=in_ap: t.transpose(b[0:npart, 0:128], i, idn[:]),
             inc=s_t)
        tdone = s_t.n
        if evac_eng == "vector":
            k.wait("vector", s_t, tdone)
            k.op("vector", lambda v, b=bank, o=out_ap: v.tensor_copy(o, b[0:npart, 0:128]),
                 inc=s_v)
            tp_free[par] = (s_v, s_v.n)
        else:
            k.wait("scalar", s_t, tdone)
            k.op("scalar", lambda s, b=bank, o=out_ap: s.copy(o, b[0:npart, 0:128]), inc=s_a)
            tp_free[par] = (s_a, s_a.n)
        return tp_free[par]

    proj_bank = {"free": [(s_v, 0), (s_v, 0)]}

    def emit_proj_block(blk, is_own):
        t0 = blk * 512
        sl = slice(t0, t0 + 512)
        mts = (("qk", wqk), ("bd", wbd), ("bb2", wbb), ("v", wv))
        for mi, (mname, wsb) in enumerate(mts):
            par = mi % 2
            bank = pA0 if par == 0 else pA1
            sem, val = proj_bank["free"][par]
            k.wait("tensor", sem, val)
            if blk == 0 and mi == 0:
                k.wait("tensor", s_ld, ld_small)
            for dt in range(NDT):
                k.op("tensor", lambda t, w=wsb, dt=dt, b=bank: t.matmul(
                    b[:], w[:, dt * 128:(dt + 1) * 128],
                    xnT[:, dt * 512:(dt + 1) * 512],
                    start=(dt == 0), stop=(dt == NDT - 1)),
                    inc=(s_t if dt == NDT - 1 else None))
            mm_done = s_t.n
            if mname == "qk":
                k.wait("vector", s_t, mm_done)
                k.op("vector", lambda v, b=bank, sl=sl: v.tensor_copy(qkf[:, sl], b[:]),
                     inc=s_v)
                proj_bank["free"][par] = (s_v, s_v.n)
            elif mname == "v":
                k.wait("vector", s_t, mm_done)
                k.op("vector", lambda v, b=bank, sl=sl: v.tensor_copy(vfm[:, sl], b[:]),
                     inc=s_v)
                proj_bank["free"][par] = (s_v, s_v.n)
            elif mname == "bd":
                k.wait("scalar", s_t, mm_done)
                k.wait("scalar", s_v, bias_ready)
                k.wait("scalar", s_a, s_a.n)
                k.op("scalar", lambda s, b=bank: s.activation(
                    escr[0:64, 0:512], b[0:64, :], AF.Exp, bias=bias1[0:64, :], scale=-1.0),
                    inc=s_a)
                k.wait("scalar", s_a, s_a.n)
                k.op("scalar", lambda s, sl=sl: s.activation(
                    nlb2[0:64, sl], escr[0:64, 0:512], AF.Ln, bias=1.0), inc=s_a)
                k.wait("scalar", s_a, s_a.n)
                k.op("scalar", lambda s, b=bank: s.activation(
                    escr[64:128, 0:512], b[64:128, :], AF.Exp, bias=bias1[64:128, :],
                    scale=1.0), inc=s_a)
                k.wait("scalar", s_a, s_a.n)
                k.op("scalar", lambda s, sl=sl: s.activation(
                    dd[64:128, sl], escr[64:128, 0:512], AF.Ln, bias=1.0), inc=s_a)
                proj_bank["free"][par] = (s_a, s_a.n)
            else:
                k.wait("scalar", s_t, mm_done)
                k.wait("scalar", s_a, s_a.n)
                k.op("scalar", lambda s, b=bank: s.activation(
                    escr[64:128, 0:512], b[64:128, :], AF.Exp, bias=bias2[64:128, :],
                    scale=-1.0), inc=s_a)
                k.wait("scalar", s_a, s_a.n)
                k.op("scalar", lambda s, sl=sl: s.activation(
                    nlb2[64:128, sl], escr[64:128, 0:512], AF.Ln, bias=1.0), inc=s_a)
                proj_bank["free"][par] = (s_a, s_a.n)

    pair_info = {}
    mt_bank = {"free": [(s_v, 0), (s_v, 0)]}
    oT_free = {"m": (s_v, 0)}
    pP_free = {"m": (s_v, 0)}
    chain = {"ready": 0, "exps_done": 0}

    def emit_pair_indep(p, is_own):
        cols = slice(p * 128, (p + 1) * 128)
        ring = p % 2
        rsl = slice(ring * 128, (ring + 1) * 128)
        tm_ev = emit_transpose(kpf[:, cols], kptm[:, cols], 128)
        kpf0_ev = None
        if is_own:
            k.wait("tensor", tm_ev[0], tm_ev[1])
            kpf0_ev = emit_transpose(kptm[:, p * 128 + 64:p * 128 + 128], kpf0[0:64, cols], 64)
        vtm_ev = emit_transpose(vfm[:, cols], vtm[:, cols], 128)
        mt_ev = None
        if is_own:
            par = p % 2
            bank = pM0 if par == 0 else pM1
            sem, val = mt_bank["free"][par]
            k.wait("tensor", sem, val)
            k.wait("tensor", kpf0_ev[0], kpf0_ev[1])
            k.op("tensor", lambda t, b=bank: t.matmul(b[:, 0:128], kpf0[0:64, cols],
                                                      qp[0:64, cols], start=True, stop=True),
                 inc=s_t)
            mm_done = s_t.n
            k.wait("vector", s_t, mm_done)
            k.wait("vector", s_cst, consts_ld)
            k.op("vector", lambda v, b=bank: v.tensor_tensor(mtm[:, rsl], b[:, 0:128],
                                                             msk[:], OP.mult), inc=s_v)
            mt_bank["free"][par] = (s_v, s_v.n)
            mt_ev = s_v.n
        pair_info[p] = dict(mt_ev=mt_ev, tm_ev=tm_ev, vtm_ev=vtm_ev)

    def emit_pair_dep(p, is_own):
        cols = slice(p * 128, (p + 1) * 128)
        ring = p % 2
        info = pair_info[p]
        vsem, vval = info["vtm_ev"]
        tsem, tval = info["tm_ev"]
        if is_own:
            sem, val = oT_free["m"]
            k.wait("tensor", sem, val)
            k.wait("tensor", vsem, vval)
            k.wait("tensor", s_v, info["mt_ev"])
            k.op("tensor", lambda t: t.matmul(pOT[:, 0:128], vtm[:, cols],
                                              mtm[:, ring * 128:ring * 128 + 128],
                                              start=True, stop=False))
        for half in range(2):
            hsl = slice(p * 128 + half * 64, p * 128 + (half + 1) * 64)
            r0, r1 = half * 64, (half + 1) * 64
            if is_own:
                k.wait("tensor", s_st, chain["ready"])
                k.op("tensor", lambda t, hsl=hsl, half=half: t.matmul(
                    pOT[:, half * 64:half * 64 + 64], Sb[:], qp[0:64, hsl],
                    start=False, stop=(half == 1)), inc=(s_t if half == 1 else None))
            sem, val = pP_free["m"]
            k.wait("tensor", sem, val)
            k.wait("tensor", vsem, vval)
            k.wait("tensor", tsem, tval)
            if not is_own:
                k.wait("tensor", s_st, chain["ready"])
            k.op("tensor", lambda t, r0=r0, r1=r1: t.matmul(
                pP[0:64, 0:128], kptm[r0:r1, p * 128 + 64:p * 128 + 128],
                vtm[r0:r1, cols], start=True, stop=True), inc=s_t)
            pmm = s_t.n
            k.wait("vector", s_t, pmm)
            k.wait("vector", s_a, chain["exps_done"])
            k.op("vector", lambda v: v.tensor_tensor(Stmp[:], Sf[:], pP[0:64, 0:128], OP.add),
                 inc=s_v)
            pP_free["m"] = (s_v, s_v.n)
            k.wait("vector", s_v, s_v.n)
            k.op("vector", lambda v, p=p, half=half: v.tensor_scalar(
                Sf[:], Stmp[:], acb[:, 2 * p + half:2 * p + half + 1], None, OP.mult),
                 inc=s_v)
            k.wait("vector", s_v, s_v.n)
            k.op("vector", lambda v: v.tensor_copy(Sb[:], Sf[:]), inc=s_st)
            chain["ready"] = s_st.n
        if is_own:
            ot_done = s_t.n
            k.wait("vector", s_t, ot_done)
            k.op("vector", lambda v: v.tensor_copy(oT[:, cols], pOT[:, 0:128]), inc=s_v)
            oT_free["m"] = (s_v, s_v.n)

    xapply = {}
    xnb_free = [(s_v, 0), (s_v, 0)]

    def emit_segment(x_src, is_own):
        for tau in range(NT):
            par = tau % 2
            if tau >= 2:
                k.wait("scalar", s_a, xapply[tau - 2])
            k.dma("scalar", xt[:, par * D:(par + 1) * D],
                  x_src[tau * 128:(tau + 1) * 128, :], s_xt[par])
            ld = s_xt[par].n
            k.wait("vector", s_xt[par], ld)
            k.wait("scalar", s_xt[par], ld)
            a_done = emit_ln_stats_apply(xt[:, par * D:(par + 1) * D],
                                         xnb[:, par * D:(par + 1) * D],
                                         dst_free=xnb_free[par])
            xapply[tau] = a_done
            k.wait("tensor", s_a, a_done)
            bi = tau % 4
            for dt in range(NDT):
                emit_transpose(xnb[:, par * D + dt * 128:par * D + (dt + 1) * 128],
                               xnT[:, dt * 512 + bi * 128:dt * 512 + bi * 128 + 128], 128,
                               evac_eng=("vector" if dt % 2 == 0 else "scalar"))
            xnb_free[par] = (s_t, s_t.n)
            if bi == 3:
                k.wait("tensor", s_v, s_v.n)
                k.wait("tensor", s_a, s_a.n)
                emit_proj_block(tau // 4, is_own)
        k.wait("vector", s_a, s_a.n)
        k.wait("vector", s_v, s_v.n)
        for ch in range(2 * NP):
            k.op("vector", lambda v, o=ch * 64: v.tensor_tensor_scan(
                scum[:, o:o + 64], nlb2[:, o:o + 64], nlb2[:, o:o + 64],
                0.0, OP.add, OP.bypass), inc=(s_v if ch == 2 * NP - 1 else None))
        scan_done = s_v.n
        k.wait("scalar", s_v, scan_done)
        if is_own:
            k.op("scalar", lambda s: s.activation(aa[0:64, :], scum[0:64, :], AF.Exp,
                                                  scale=-1.0))
        k.op("scalar", lambda s: s.activation(aa[64:128, :], scum[64:128, :], AF.Exp,
                                              scale=1.0))
        scum_last = scum.rearrange("p (ch c) -> p ch c", c=64)
        k.op("scalar", lambda s: s.activation(acb[:, 0:32], scum_last[0:64, :, 63], AF.Exp,
                                              scale=-1.0), inc=s_a)
        exps_done = s_a.n
        chain["exps_done"] = exps_done
        k.wait("vector", s_a, exps_done)
        if is_own:
            k.op("vector", lambda v: v.tensor_tensor(qp[0:64, :], qkf[0:64, :],
                                                     aa[0:64, :], OP.mult))
        k.op("vector", lambda v: v.tensor_tensor(kpw[64:128, :], qkf[64:128, :],
                                                 dd[64:128, :], OP.mult), inc=s_v)
        k.wait("vector", s_v, s_v.n)
        k.op("vector", lambda v: v.tensor_tensor(kpf[64:128, :], kpw[64:128, :],
                                                 aa[64:128, :], OP.mult), inc=s_v)
        elt_done = s_v.n
        k.wait("tensor", s_v, elt_done)
        for p in range(NP + 1):
            if p < NP:
                emit_pair_indep(p, is_own)
            if p > 0:
                emit_pair_dep(p - 1, is_own)

    k.wait("vector", s_v, memset_done)
    emit_segment(xp_d, is_own=False)
    emit_segment(xo_d, is_own=True)

    x1_written = []
    att_ready = 0
    a4_xt = {}
    for tau in range(NT):
        par = tau % 2
        for nb in range(4):
            bpar = nb % 2
            bank = pA0 if bpar == 0 else pA1
            sem, val = proj_bank["free"][bpar]
            k.wait("tensor", sem, val)
            if tau == 0 and nb == 0:
                sem2, val2 = oT_free["m"]
                k.wait("tensor", sem2, val2)
            k.op("tensor", lambda t, nb=nb, tau=tau, b=bank: t.matmul(
                b[:], oT[:, tau * 128:(tau + 1) * 128], wo[:, nb * 512:(nb + 1) * 512],
                start=True, stop=True), inc=s_t)
            mm_done = s_t.n
            k.wait("vector", s_t, mm_done)
            if att_ready:
                k.wait("vector", s_a, att_ready)
            else:
                k.wait("vector", s_a, chain["exps_done"])
            k.op("vector", lambda v, nb=nb, b=bank: v.tensor_copy(
                att[:, nb * 512:(nb + 1) * 512], b[:]), inc=s_v)
            proj_bank["free"][bpar] = (s_v, s_v.n)
        att_done = s_v.n
        if tau >= 2:
            k.wait("scalar", s_v, a4_xt[tau - 2])
        else:
            k.wait("scalar", s_a, xapply[NT - 2 + tau])
        k.dma("scalar", xt[:, par * D:(par + 1) * D], xo_d[tau * 128:(tau + 1) * 128, :],
              s_xt[par])
        xld = s_xt[par].n
        k.wait("vector", s_v, att_done)
        k.wait("scalar", s_v, att_done)
        if x1_written:
            k.wait("scalar", s_x1w, x1_written[-1])
        a_done = emit_ln_stats_apply(att[:, 0:D], x1t[:, 0:D])
        att_ready = a_done
        k.wait("vector", s_a, a_done)
        k.wait("vector", s_xt[par], xld)
        k.op("vector", lambda v, par=par: v.tensor_tensor(
            x1t[:, 0:D], x1t[:, 0:D], xt[:, par * D:(par + 1) * D], OP.add), inc=s_v)
        x1_done = s_v.n
        a4_xt[tau] = x1_done

        k.wait("scalar", s_v, x1_done)
        k.dma("scalar", x1s[tau * 128:(tau + 1) * 128, :], x1t[:, 0:D], s_x1w)
        x1_written.append(s_x1w.n)
        k.wait("vector", s_v, x1_done)
        k.wait("scalar", s_v, x1_done)
        a2_done = emit_ln_stats_apply(x1t[:, 0:D], xnb[:, par * D:(par + 1) * D],
                                      dst_free=xnb_free[par])
        k.wait("tensor", s_a, a2_done)
        for dt in range(NDT):
            emit_transpose(xnb[:, par * D + dt * 128:par * D + (dt + 1) * 128],
                           xn2T[:, dt * TH + tau * 128:dt * TH + tau * 128 + 128], 128,
                           evac_eng=("vector" if dt % 2 == 0 else "scalar"))
        xnb_free[par] = (s_t, s_t.n)

    x1_all_written = s_x1w.n
    xn2T_v, xn2T_a = s_v.n, s_a.n

    k.barrier()
    al.free_from(amark)
    bmark = al.mark()
    w1n = al.sb("w1n_b", [128, 2 * D], BF16)
    w3n = al.sb("w3n_b", [128, 2 * D], BF16)
    w1t = al.sb("w1t_b", [128, 2 * D], BF16)
    w3t = al.sb("w3t_b", [128, 2 * D], BF16)
    hid = al.sb("hid_b", [128, 2 * 512], BF16)
    zbr = al.sb("zbr_b", [128, 2 * TH], BF16)

    w1r = w1_d.rearrange("(h1 j c) d -> c j h1 d", h1=64, j=2, c=64)
    w3r = w3_d.rearrange("(h1 j c) d -> c j h1 d", h1=64, j=2, c=64)
    import collections as _c
    wtq = {"dt": _c.defaultdict(int), "done": {}}

    def emit_wtrans(c):
        par = c % 2
        dt = wtq["dt"][c]
        wtq["dt"][c] += 1
        emit_transpose(w1n[:, par * D + dt * 128:par * D + (dt + 1) * 128],
                       w1t[:, par * D + dt * 128:par * D + (dt + 1) * 128], 128,
                       evac_eng=("vector" if dt % 2 == 0 else "scalar"))
        emit_transpose(w3n[:, par * D + dt * 128:par * D + (dt + 1) * 128],
                       w3t[:, par * D + dt * 128:par * D + (dt + 1) * 128], 128,
                       evac_eng=("scalar" if dt % 2 == 0 else "vector"))
        if wtq["dt"][c] == NDT:
            wtq["done"][c] = (s_t, s_t.n)

    def emit_wload(c):
        par = c % 2
        dsl = slice(par * D, (par + 1) * D)
        if c >= 2:
            sem, val = wtq["done"][c - 2]
            k.wait("gpsimd", sem, val)
        k.dma("gpsimd", w1n[:, dsl], w1r[c], s_wl[par])
        k.dma("gpsimd", w3n[:, dsl], w3r[c], s_wl[par])

    emit_wload(0)
    emit_wload(1)

    wtrans_done = []
    zb_written = []
    zb_count = [0, 0]
    bst = {"ug": (s_v, 0), "ul": (s_v, 0), "zb": (s_v, 0),
           "hid": [(s_v, 0), (s_v, 0)], "pend_bp": []}

    def emit_bp(job):
        hsl, tsl, par, mul_done, hpar = job
        sem, val = bst["zb"]
        k.wait("tensor", sem, val)
        k.wait("tensor", s_v, mul_done)
        k.op("tensor", lambda t, hsl=hsl: t.matmul(pM1[:], hb1[:], hid[:, hsl],
                                                   start=True, stop=True), inc=s_t)
        zb_mm = s_t.n
        bst["hid"][hpar] = (s_t, zb_mm)
        k.wait("vector", s_t, zb_mm)
        k.op("vector", lambda v, tsl=tsl, par=par: v.tensor_copy(
            zbr[:, par * TH + tsl.start:par * TH + tsl.stop], pM1[:]), inc=s_v)
        bst["zb"] = (s_v, s_v.n)
        zb_count[par] += 1
        if zb_count[par] % 4 == 0:
            cc = (zb_count[par] - 1) // 4 * 2 + par
            k.wait("sync", s_v, s_v.n)
            k.dma("sync", zbs[:, cc, :], zbr[:, par * TH:(par + 1) * TH], s_zbw[par])
            zb_written[cc] = s_zbw[par].n

    for c in range(NC):
        par = c % 2
        if c == 0:
            k.wait("tensor", s_wl[0], s_wl[0].n)
            k.wait("tensor", s_v, xn2T_v)
            k.wait("tensor", s_a, xn2T_a)
        while wtq["dt"][c] < NDT:
            emit_wtrans(c)
        if c == 0:
            emit_wload(2)
        k.wait("tensor", s_v, s_v.n)
        k.wait("tensor", s_a, s_a.n)
        if c >= 2:
            assert zb_written[c - 2] is not None
            k.wait("vector", s_zbw[par], zb_written[c - 2])

        for ts in range(4):
            tsl = slice(ts * 512, (ts + 1) * 512)
            hpar = ts % 2
            hsl = slice(hpar * 512, (hpar + 1) * 512)
            sem, val = bst["ug"]
            k.wait("tensor", sem, val)
            for dt in range(NDT):
                k.op("tensor", lambda t, dt=dt, ts=ts, par=par: t.matmul(
                    pA0[:], w1t[:, par * D + dt * 128:par * D + (dt + 1) * 128],
                    xn2T[:, dt * TH + ts * 512:dt * TH + ts * 512 + 512],
                    start=(dt == 0), stop=(dt == NDT - 1)),
                    inc=(s_t if dt == NDT - 1 else None))
            ug_done = s_t.n
            sem, val = bst["ul"]
            k.wait("tensor", sem, val)
            for dt in range(NDT):
                k.op("tensor", lambda t, dt=dt, ts=ts, par=par: t.matmul(
                    pA1[:], w3t[:, par * D + dt * 128:par * D + (dt + 1) * 128],
                    xn2T[:, dt * TH + ts * 512:dt * TH + ts * 512 + 512],
                    start=(dt == 0), stop=(dt == NDT - 1)),
                    inc=(s_t if dt == NDT - 1 else None))
            ul_done = s_t.n
            if c + 1 < NC:
                if wtq["dt"][c + 1] == 0:
                    k.wait("tensor", s_wl[(c + 1) % 2], s_wl[(c + 1) % 2].n)
                for _ in range(4):
                    if wtq["dt"][c + 1] < NDT:
                        emit_wtrans(c + 1)
                if wtq["dt"][c + 1] == NDT and c + 3 < NC:
                    emit_wload(c + 3)
            sem, val = bst["hid"][hpar]
            k.wait("scalar", sem, val)
            k.wait("scalar", s_t, ug_done)
            k.op("scalar", lambda s, hsl=hsl: s.activation(hid[:, hsl], pA0[:], AF.Silu),
                 inc=s_a)
            silu_done = s_a.n
            bst["ug"] = (s_a, silu_done)
            k.wait("vector", s_a, silu_done)
            k.wait("vector", s_t, ul_done)
            k.op("vector", lambda v, hsl=hsl: v.tensor_tensor(hid[:, hsl], hid[:, hsl],
                                                              pA1[:], OP.mult), inc=s_v)
            mul_done = s_v.n
            bst["ul"] = (s_v, mul_done)
            bst["pend_bp"].append((hsl, tsl, par, mul_done, hpar))
            if len(bst["pend_bp"]) > 1:
                emit_bp(bst["pend_bp"].pop(0))
        wtrans_done.append((s_t, s_t.n))
        zb_written.append(None)
        if c == NC - 1:
            while bst["pend_bp"]:
                emit_bp(bst["pend_bp"].pop(0))

    k.barrier()
    al.free_from(bmark)
    al.free_from(xn2T_mark)
    al.free_from(aw_mark)
    cmark = al.mark()
    ZCH = 4
    GR = 4
    gbuf = al.sb("g_b", [128, GR * TH], BF16)
    zfr = al.sb("zfr_b", [128, 2 * TH], BF16)
    zfl = al.sb("zfl_b", [128, 2 * ZCH * 512], BF16)
    w2t = al.sb("w2t_b", [128, 2 * 64 * 512], BF16)
    otb = al.sb("otb_b", [128, 4 * 512], BF16)

    zb_view = zbs.rearrange("(j q) c t -> q j c t", j=2)
    zfs_view = zfs.rearrange("g p t -> p g t")

    gfree = [(s_v, 0)] * GR
    zfr_free = [(s_v, 0)] * 2
    banks4 = (pA0, pA1, pM0, pOT)
    cbank = [(s_v, 0)] * 4
    k.wait("sync", s_zbw[0], s_zbw[0].n)
    k.wait("sync", s_zbw[1], s_zbw[1].n)
    for g1 in range(64):
        r = g1 % GR
        r2 = g1 % 2
        gb = slice(r * TH, (r + 1) * TH)
        zb_ = slice(r2 * TH, (r2 + 1) * TH)
        sem, val = gfree[r]
        k.wait("sync", sem, val)
        k.dma("sync", gbuf[:, gb], zb_view[g1], s_gg[r])
        gld = s_gg[r].n
        psem, pval = zfr_free[r2]
        k.wait("vector", psem, pval)
        k.wait("tensor", s_gg[r], gld)
        for i in range(4):
            bank = banks4[i]
            sem, val = cbank[i]
            k.wait("tensor", sem, val)
            k.op("tensor", lambda t, o=gb.start + i * 512, b=bank: t.matmul(
                b[:], hb2[:], gbuf[:, o:o + 512], start=True, stop=True), inc=s_t)
            cmms = s_t.n
            k.wait("vector", s_t, cmms)
            k.op("vector", lambda v, o=zb_.start + i * 512, b=bank: v.tensor_scalar(
                zfr[:, o:o + 512], b[:], RS, None, OP.mult), inc=s_v)
            cbank[i] = (s_v, s_v.n)
        gfree[r] = (s_t, s_t.n)
        ev = s_v.n
        k.wait("sync", s_v, ev)
        k.dma("sync", zfs[g1], zfr[:, zb_], s_zfw[g1 % 4])
        zfr_free[r2] = (s_zfw[g1 % 4], s_zfw[g1 % 4].n)

    banksC = (pM0, pOT, pP, pM1)
    w2t_ld = [None] * 4
    grp_mm_done = [None] * 4

    def emit_w2t_loads(grp):
        gpar = grp % 2
        base = gpar * 64 * 512
        if grp >= 2:
            sem, val = grp_mm_done[grp - 2]
            k.wait("scalar", sem, val)
        if grp == 0:
            k.wait("scalar", s_w2c, w2cast_done)
        for a1 in range(64):
            k.dmat("scalar", w2t[:, base + a1 * 512:base + (a1 + 1) * 512],
                   w2s[grp * 512:(grp + 1) * 512, a1 * 128:(a1 + 1) * 128], s_w2t[gpar])
        w2t_ld[grp] = s_w2t[gpar].n

    emit_w2t_loads(0)
    emit_w2t_loads(1)

    zfl_free = [(s_v, 0)] * 2
    cmb_free = [cbank[2], cbank[3], (s_v, 0), (s_v, 0)]
    otb_gate = 0
    for r in range(4):
        k.wait("sync", s_zfw[r], s_zfw[r].n)
    ffs_dT_view = ffs_dT.rearrange("(a p) t -> p a t", p=128)
    for grp in range(4):
        gpar = grp % 2
        base = gpar * 64 * 512
        k.wait("tensor", s_w2t[grp % 2], w2t_ld[grp])
        for ts in range(4):
            for bi in range(4):
                sem, val = cmb_free[bi]
                k.wait("tensor", sem, val)
            for zc in range(64 // ZCH):
                r = zc % 2
                a1a = zc * ZCH
                base_r = r * ZCH * 512
                sem, val = zfl_free[r]
                k.wait("sync", sem, val)
                k.dma("sync", zfl[:, base_r:base_r + ZCH * 512].rearrange(
                    "p (q t) -> p q t", q=ZCH),
                    zfs_view[:, a1a:a1a + ZCH, ts * 512:(ts + 1) * 512], s_zfl[r])
                zld = s_zfl[r].n
                k.wait("tensor", s_zfl[r], zld)
                for i in range(ZCH):
                    a1 = a1a + i
                    for dt in range(4):
                        k.op("tensor", lambda t, a1=a1, dt=dt, o=base_r + i * 512,
                             base=base: t.matmul(
                            banksC[dt][:],
                            w2t[:, base + a1 * 512 + dt * 128:base + a1 * 512 + (dt + 1) * 128],
                            zfl[:, o:o + 512], start=(a1 == 0), stop=(a1 == 63)),
                            inc=(s_t if dt == 3 else None))
                zfl_free[r] = (s_t, s_t.n)
            acc_done = s_t.n
            k.wait("vector", s_t, acc_done)
            if otb_gate:
                k.wait("vector", s_ffw, otb_gate)
            for dt in range(4):
                k.op("vector", lambda v, dt=dt: v.tensor_copy(
                    otb[:, dt * 512:(dt + 1) * 512], banksC[dt][:]), inc=s_v)
                cmb_free[dt] = (s_v, s_v.n)
            evd = s_v.n
            k.wait("sync", s_v, evd)
            k.dma("sync", ffs_dT_view[:, grp * 4:(grp + 1) * 4, ts * 512:(ts + 1) * 512],
                  otb[:].rearrange("p (q t) -> p q t", q=4), s_ffw)
            otb_gate = s_ffw.n
        grp_mm_done[grp] = (s_t, s_t.n)
        if grp + 2 < 4:
            emit_w2t_loads(grp + 2)
    ffs_all = s_ffw.n

    k.barrier()
    al.free_from(cmark)
    fft = al.sb("fft_b", [128, 2 * D], BF16)
    out_res = []
    out_dma = []
    for tau in range(NT):
        par = tau % 2
        dsl = slice(par * D, (par + 1) * D)
        if tau == 0:
            k.wait("scalar", s_ffw, ffs_all)
            k.wait("scalar", s_x1w, x1_all_written)
        if tau >= 2:
            k.wait("scalar", s_v, out_res[tau - 2])
        k.dmat("scalar", fft[:, dsl], ffs_dT[:, tau * 128:(tau + 1) * 128], s_ffl[par])
        k.dma("scalar", xt[:, dsl], x1s[tau * 128:(tau + 1) * 128, :], s_ffl[par])
        ld = s_ffl[par].n
        if tau >= 1:
            k.wait("scalar", s_out, out_dma[tau - 1])
        k.wait("vector", s_ffl[par], ld)
        k.wait("scalar", s_ffl[par], ld)
        a_done = emit_ln_stats_apply(fft[:, dsl], x1t[:, 0:D])
        k.wait("vector", s_a, a_done)
        k.op("vector", lambda v, dsl=dsl: v.tensor_tensor(x1t[:, 0:D], x1t[:, 0:D],
                                                          xt[:, dsl], OP.add), inc=s_v)
        out_res.append(s_v.n)
        k.wait("sync", s_v, s_v.n)
        k.dma("sync", out_d[tau * 128:(tau + 1) * 128, :], x1t[:, 0:D], s_out)
        out_dma.append(s_out.n)
    k.wait("sync", s_out, s_out.n)

    with nc.Block() as block:
        k.build(block)
    k.close_sems()
    for ctx in reversed(psum_ctxs):
        ctx.__exit__(None, None, None)
    al.free_from(0)
    return nc


def _consts():
    idx = np.arange(64)
    pc = np.zeros((64, 64), dtype=np.int64)
    for i in range(64):
        pc[i] = [bin(i & j).count("1") for j in range(64)]
    H64 = np.where(pc % 2 == 0, 1.0, -1.0).astype(np.float32)
    H2 = np.array([[1.0, 1.0], [1.0, -1.0]], dtype=np.float32)
    hb1 = np.einsum("ab,cd->acbd", H2, H64).reshape(128, 128)
    hb2 = np.zeros((128, 128), dtype=np.float32)
    hb2[0:64, 0:64] = H64
    hb2[64:128, 64:128] = H64
    tri = np.triu(np.ones((64, 64), dtype=np.float32))
    mask = np.zeros((128, 128), dtype=np.float32)
    mask[0:64, 0:64] = tri
    mask[64:128, 64:128] = tri
    bf = ml_dtypes.bfloat16
    return (hb1.astype(bf), hb2.astype(bf), mask.astype(bf), np.eye(128, dtype=bf))


def kernel(L0_attn_x, L0_attn_Wq, L0_attn_Wk, L0_attn_Wv, L0_attn_Wo,
           L0_attn_Wb, L0_attn_bb, L0_attn_Wd, L0_attn_bd,
           L0_ffn_W1, L0_ffn_W2, L0_ffn_W3):
    x = np.ascontiguousarray(np.asarray(L0_attn_x, dtype=np.float32))
    wqk = np.ascontiguousarray(np.concatenate([L0_attn_Wq, L0_attn_Wk], axis=1).astype(np.float32))
    wbd = np.ascontiguousarray(np.concatenate([L0_attn_Wb, L0_attn_Wd], axis=1).astype(np.float32))
    wbb = np.ascontiguousarray(np.concatenate([L0_attn_Wb, L0_attn_Wb], axis=1).astype(np.float32))
    wv = np.ascontiguousarray(np.asarray(L0_attn_Wv, dtype=np.float32))
    wo = np.ascontiguousarray(np.asarray(L0_attn_Wo, dtype=np.float32))
    bb = np.ascontiguousarray(np.asarray(L0_attn_bb, dtype=np.float32).reshape(64, 1))
    bd = np.ascontiguousarray(np.asarray(L0_attn_bd, dtype=np.float32).reshape(64, 1))
    w1 = np.ascontiguousarray(np.asarray(L0_ffn_W1, dtype=np.float32))
    w2 = np.ascontiguousarray(np.asarray(L0_ffn_W2, dtype=np.float32))
    w3 = np.ascontiguousarray(np.asarray(L0_ffn_W3, dtype=np.float32))
    hb1, hb2, mask, idn = _consts()
    zeros = np.zeros((TH, D), dtype=np.float32)

    nc = build_nc()
    in_maps = []
    for core in range(8):
        b, half = core // 2, core % 2
        xo = np.ascontiguousarray(x[b, half * TH:(half + 1) * TH, :])
        xp = np.ascontiguousarray(x[b, 0:TH, :]) if half == 1 else zeros
        in_maps.append({
            "xo": xo, "xp": xp, "wqk": wqk, "wbd": wbd, "wbb": wbb, "wv": wv,
            "wo": wo, "bb": bb, "bd": bd, "w1": w1, "w2": w2, "w3": w3,
            "hb1": hb1, "hb2": hb2, "msk": mask, "idn": idn,
        })
    res = run_bass_kernel_spmd(nc, in_maps, core_ids=list(range(8)))
    out = np.empty((4, 4096, 2048), dtype=np.float32)
    for core in range(8):
        b, half = core // 2, core % 2
        out[b, half * TH:(half + 1) * TH, :] = res.results[core]["out"]
    return out


# version 09
# speedup vs baseline: 1.1218x; 1.1050x over previous
import numpy as np
import ml_dtypes
import concourse.bass as bass
import concourse.mybir as mybir
from concourse.bass_utils import run_bass_kernel_spmd
from concourse.alu_op_type import AluOpType as OP

F32 = mybir.dt.float32
BF16 = mybir.dt.bfloat16
AF = mybir.ActivationFunctionType
AX = mybir.AxisListType

TH = 2048
D = 2048
H = 8192
NT = TH // 128
NP = TH // 128
NC = 64
NDT = D // 128
RS = float(1.0 / np.sqrt(8192.0))

ENGINES = ("tensor", "vector", "scalar", "gpsimd", "sync")


class Sem:
    __slots__ = ("h", "n", "name")

    def __init__(self, h, name):
        self.h = h
        self.n = 0
        self.name = name


class K:
    def __init__(self, nc):
        self.nc = nc
        self.items = {e: [] for e in ENGINES}
        self.sem_ctxs = []
        self.sems = []

    def sem(self, name):
        ctx = self.nc.semaphore(name)
        h = ctx.__enter__()
        self.sem_ctxs.append(ctx)
        s = Sem(h, name)
        self.sems.append(s)
        return s

    def wait(self, eng, sem, val):
        assert val <= sem.n, f"wait on {sem.name} for {val} but only {sem.n} emitted"
        if val > 0:
            self.items[eng].append(("w", sem, val))

    def op(self, eng, fn, inc=None, amt=1):
        self.items[eng].append(("o", fn, inc, amt))
        if inc is not None:
            inc.n += amt
            return inc.n
        return None

    def dma(self, eng, dst, src, sem, **kw):
        return self.op(eng, lambda e: e.dma_start(dst, src, **kw), inc=sem, amt=16)

    def dmat(self, eng, dst, src, sem):
        return self.op(eng, lambda e: e.dma_start_transpose(dst, src), inc=sem, amt=16)

    def barrier(self):
        snap = [(s, s.n) for s in self.sems if s.n > 0]
        for e in ENGINES:
            for s, v in snap:
                self.items[e].append(("w", s, v))

    def build(self, block):
        for eng in ENGINES:
            dec = getattr(block, eng)
            items = self.items[eng]

            def make(items=items):
                def body(h):
                    for it in items:
                        if it[0] == "w":
                            h.wait_ge(it[1].h, it[2])
                        else:
                            inst = it[1](h)
                            if it[2] is not None:
                                inst.then_inc(it[2].h, it[3])
                return body
            dec(make())

    def close_sems(self):
        for ctx in reversed(self.sem_ctxs):
            ctx.__exit__(None, None, None)


class Alloc:
    def __init__(self, nc):
        self.nc = nc
        self.stack = []

    def sb(self, name, shape, dtype):
        ctx = self.nc.sbuf_tensor(name, shape, dtype)
        h = ctx.__enter__()
        self.stack.append((name, ctx))
        return h

    def free_from(self, mark):
        while len(self.stack) > mark:
            name, ctx = self.stack.pop()
            ctx.__exit__(None, None, None)

    def mark(self):
        return len(self.stack)


def build_nc():
    nc = bass.Bass("TRN2", target_bir_lowering=False)

    xo_d = nc.dram_tensor("xo", [TH, D], F32, kind="ExternalInput").ap()
    xp_d = nc.dram_tensor("xp", [TH, D], F32, kind="ExternalInput").ap()
    wqk_d = nc.dram_tensor("wqk", [D, 128], F32, kind="ExternalInput").ap()
    wbd_d = nc.dram_tensor("wbd", [D, 128], F32, kind="ExternalInput").ap()
    wbb_d = nc.dram_tensor("wbb", [D, 128], F32, kind="ExternalInput").ap()
    wv_d = nc.dram_tensor("wv", [D, 128], F32, kind="ExternalInput").ap()
    wo_d = nc.dram_tensor("wo", [128, D], F32, kind="ExternalInput").ap()
    bb_d = nc.dram_tensor("bb", [64, 1], F32, kind="ExternalInput").ap()
    bd_d = nc.dram_tensor("bd", [64, 1], F32, kind="ExternalInput").ap()
    w1_d = nc.dram_tensor("w1", [H, D], F32, kind="ExternalInput").ap()
    w2_d = nc.dram_tensor("w2", [D, H], F32, kind="ExternalInput").ap()
    w3_d = nc.dram_tensor("w3", [H, D], F32, kind="ExternalInput").ap()
    hb1_d = nc.dram_tensor("hb1", [128, 128], BF16, kind="ExternalInput").ap()
    hb2_d = nc.dram_tensor("hb2", [128, 128], BF16, kind="ExternalInput").ap()
    msk_d = nc.dram_tensor("msk", [128, 128], BF16, kind="ExternalInput").ap()
    idn_d = nc.dram_tensor("idn", [128, 128], BF16, kind="ExternalInput").ap()

    out_d = nc.dram_tensor("out", [TH, D], F32, kind="ExternalOutput").ap()

    w2s = nc.dram_tensor("w2s", [D, H], BF16, kind="Internal").ap()
    x1s = nc.dram_tensor("x1s", [TH, D], F32, kind="Internal").ap()
    zbs = nc.dram_tensor("zbs", [128, NC, TH], BF16, kind="Internal").ap()
    zfs = nc.dram_tensor("zfs", [64, 128, TH], BF16, kind="Internal").ap()
    ffs_dT = nc.dram_tensor("ffs", [D, TH], BF16, kind="Internal").ap()

    al = Alloc(nc)

    idn = al.sb("idn_s", [128, 128], BF16)
    hb1 = al.sb("hb1_s", [128, 128], BF16)
    hb2 = al.sb("hb2_s", [128, 128], BF16)
    msk = al.sb("msk_s", [128, 128], BF16)
    stat = al.sb("stat_s", [128, 16], F32)
    sqs = al.sb("sqs_s", [128, D], BF16)
    xt = al.sb("xt_s", [128, 2 * D], F32)
    xnb = al.sb("xnb_s", [128, 2 * D], BF16)
    x1t = al.sb("x1t_s", [128, D], F32)
    epsb = al.sb("epsb_s", [128, 1], F32)

    aw_mark = al.mark()
    wqk = al.sb("wqk_s", [128, NDT * 128], BF16)
    wbd = al.sb("wbd_s", [128, NDT * 128], BF16)
    wbb = al.sb("wbb_s", [128, NDT * 128], BF16)
    wv = al.sb("wv_s", [128, NDT * 128], BF16)
    wo = al.sb("wo_s", [128, D], BF16)
    bias1 = al.sb("bias1_s", [128, 1], F32)
    bias2 = al.sb("bias2_s", [128, 1], F32)

    xn2T_mark = al.mark()
    xn2T = al.sb("xn2T_b", [128, NDT * TH], BF16)

    psum_ctxs = []

    def ps(name, shape, dtype):
        ctx = nc.psum_tensor(name, shape, dtype)
        h = ctx.__enter__()
        psum_ctxs.append(ctx)
        return h

    pA0 = ps("pA0", [128, 512], F32)
    pA1 = ps("pA1", [128, 512], F32)
    pT0 = ps("pT0", [128, 1024], BF16)
    pT1 = ps("pT1", [128, 1024], BF16)
    pM0 = ps("pM0", [128, 512], F32)
    pOT = ps("pOT", [128, 512], F32)
    pP = ps("pP", [128, 512], F32)
    pM1 = ps("pM1", [128, 512], F32)

    k = K(nc)
    s_cst = k.sem("s_cst")
    s_ld = k.sem("s_ld")
    s_w2c = k.sem("s_w2c")
    s_wl = [k.sem("s_wl0"), k.sem("s_wl1")]
    s_xt = [k.sem("s_xt0"), k.sem("s_xt1")]
    s_x1w = k.sem("s_x1w")
    s_zbw = [k.sem("s_zbw0"), k.sem("s_zbw1")]
    s_gg = [k.sem(f"s_gg{i}") for i in range(4)]
    s_zfw = [k.sem(f"s_zfw{i}") for i in range(4)]
    s_zfl = [k.sem(f"s_zfl{i}") for i in range(4)]
    s_w2t = [k.sem("s_w2t0"), k.sem("s_w2t1")]
    s_ffw = k.sem("s_ffw")
    s_ffl = [k.sem("s_ffl0"), k.sem("s_ffl1")]
    s_t = k.sem("s_t")
    s_v = k.sem("s_v")
    s_a = k.sem("s_a")
    s_st = k.sem("s_st")
    s_out = k.sem("s_out")

    k.dma("sync", idn[:], idn_d, s_cst)
    k.dma("sync", hb1[:], hb1_d, s_cst)
    k.dma("sync", hb2[:], hb2_d, s_cst)
    k.dma("sync", msk[:], msk_d, s_cst)
    consts_ld = s_cst.n
    for dst, src in ((wqk, wqk_d), (wbd, wbd_d), (wbb, wbb_d), (wv, wv_d)):
        k.dma("gpsimd", dst[:], src.rearrange("(dt p) f -> p dt f", p=128), s_ld)
    k.dma("gpsimd", wo[:], wo_d, s_ld)
    k.dma("gpsimd", bias1[0:64, :], bb_d, s_ld)
    k.dma("gpsimd", bias1[64:128, :], bd_d, s_ld)
    k.dma("gpsimd", bias2[64:128, :], bb_d, s_ld)
    ld_small = s_ld.n
    for ch in range(32):
        k.dma("gpsimd", w2s[ch * 64:(ch + 1) * 64, :], w2_d[ch * 64:(ch + 1) * 64, :], s_w2c)
    w2cast_done = s_w2c.n

    k.wait("vector", s_ld, ld_small)
    k.op("vector", lambda v: v.tensor_scalar(bias1[0:64, :], bias1[0:64, :], -1.0, None, OP.mult))
    k.op("vector", lambda v: v.tensor_scalar(bias2[64:128, :], bias2[64:128, :], -1.0, None,
                                             OP.mult), inc=s_v)
    k.op("vector", lambda v: v.memset(epsb[:], 1e-5), inc=s_v)
    bias_ready = s_v.n
    k.wait("tensor", s_cst, consts_ld)

    amark = al.mark()
    xnT = al.sb("xnT_b", [128, NDT * 512], BF16)
    qkf = al.sb("qkf_b", [128, TH], BF16)
    nlb2 = al.sb("nlb2_b", [128, TH], F32)
    dd = al.sb("dd_b", [128, TH], BF16)
    vfm = al.sb("vfm_b", [128, TH], BF16)
    escr = al.sb("escr_b", [128, 512], F32)
    scum = al.sb("scum_b", [128, TH], F32)
    aa = al.sb("aa_b", [128, TH], BF16)
    qp = al.sb("qp_b", [128, TH], BF16)
    kpw = al.sb("kpw_b", [128, TH], BF16)
    kpf = al.sb("kpf_b", [128, TH], BF16)
    kptm = al.sb("kptm_b", [128, TH], BF16)
    kpf0 = al.sb("kpf0_b", [128, TH], BF16)
    vtm = al.sb("vtm_b", [128, TH], BF16)
    mtm = al.sb("mtm_b", [128, 256], BF16)
    acb = al.sb("acb_b", [64, 32], F32)
    Sf = al.sb("Sf_b", [64, 128], F32)
    Sb = al.sb("Sb_b", [64, 128], BF16)
    Stmp = al.sb("Stmp_b", [64, 128], F32)
    oT = al.sb("oT_b", [128, TH], BF16)
    att = scum

    k.op("vector", lambda v: v.memset(Sf[:], 0.0))
    k.op("vector", lambda v: v.memset(Sb[:], 0.0))
    k.op("vector", lambda v: v.memset(kpf[:], 0.0))
    k.op("vector", lambda v: v.memset(qp[:], 0.0))
    k.op("vector", lambda v: v.memset(kpf0[:], 0.0), inc=s_v)
    memset_done = s_v.n

    ln_state = {"slot": 0, "free": [(s_v, 0), (s_v, 0)]}

    def emit_ln_stats_apply(src_ap, dst_ap, dst_free=None):
        sl = ln_state["slot"]
        ln_state["slot"] ^= 1
        c0 = sl * 8

        def st_(i):
            return stat[:, c0 + i:c0 + i + 1]

        fsem, fval = ln_state["free"][sl]
        k.wait("vector", fsem, fval)
        k.op("vector", lambda v: v.reduce_sum(st_(0), src_ap, axis=AX.X), inc=s_v)
        rdone = s_v.n
        k.wait("scalar", fsem, fval)
        k.op("scalar", lambda s: s.activation(sqs[:, 0:D], src_ap, AF.Square,
                                              accum_out=st_(1)), inc=s_a)
        sq_done = s_a.n
        k.wait("vector", s_v, rdone)
        k.op("vector", lambda v: v.tensor_scalar(st_(2), st_(0), 1.0 / D, None, OP.mult),
             inc=s_v)
        k.wait("vector", s_a, sq_done)
        k.wait("vector", s_v, s_v.n)
        k.op("vector", lambda v: v.tensor_scalar(st_(3), st_(1), 1.0 / D, None, OP.mult))
        k.op("vector", lambda v: v.tensor_tensor(st_(4), st_(2), st_(2), OP.mult), inc=s_v)
        k.wait("vector", s_v, s_v.n)
        k.op("vector", lambda v: v.tensor_tensor(st_(4), st_(3), st_(4), OP.subtract),
             inc=s_v)
        var_done = s_v.n
        k.wait("scalar", s_v, var_done)
        k.op("scalar", lambda s: s.activation(st_(5), st_(4), AF.Ln, bias=epsb[:, 0:1]),
             inc=s_a)
        k.wait("scalar", s_a, s_a.n)
        k.op("scalar", lambda s: s.activation(st_(6), st_(5), AF.Exp, scale=-0.5), inc=s_a)
        rstd_done = s_a.n
        k.wait("vector", s_a, rstd_done)
        k.op("vector", lambda v: v.scalar_tensor_tensor(st_(7), st_(2), -1.0, st_(6),
                                                        OP.mult, OP.mult), inc=s_v)
        nmr_done = s_v.n
        k.wait("scalar", s_v, nmr_done)
        if dst_free is not None:
            k.wait("scalar", dst_free[0], dst_free[1])
        k.op("scalar", lambda s: s.activation(dst_ap, src_ap, AF.Identity,
                                              bias=st_(7), scale=st_(6)), inc=s_a)
        ln_state["free"][sl] = (s_a, s_a.n)
        return s_a.n

    tp_state = {"n": 0}
    tp_free = [(s_v, 0), (s_v, 0)]

    def emit_transpose(in_ap, out_ap, npart, evac_eng="vector"):
        par = tp_state["n"] % 2
        tp_state["n"] += 1
        bank = pT0 if par == 0 else pT1
        sem, val = tp_free[par]
        k.wait("tensor", sem, val)
        k.op("tensor", lambda t, b=bank, i=in_ap: t.transpose(b[0:npart, 0:128], i, idn[:]),
             inc=s_t)
        tdone = s_t.n
        if evac_eng == "vector":
            k.wait("vector", s_t, tdone)
            k.op("vector", lambda v, b=bank, o=out_ap: v.tensor_copy(o, b[0:npart, 0:128]),
                 inc=s_v)
            tp_free[par] = (s_v, s_v.n)
        else:
            k.wait("scalar", s_t, tdone)
            k.op("scalar", lambda s, b=bank, o=out_ap: s.copy(o, b[0:npart, 0:128]), inc=s_a)
            tp_free[par] = (s_a, s_a.n)
        return tp_free[par]

    proj_bank = {"free": [(s_v, 0), (s_v, 0)]}

    def emit_proj_block(blk, is_own):
        t0 = blk * 512
        sl = slice(t0, t0 + 512)
        mts = (("qk", wqk), ("bd", wbd), ("bb2", wbb), ("v", wv))
        for mi, (mname, wsb) in enumerate(mts):
            par = mi % 2
            bank = pA0 if par == 0 else pA1
            sem, val = proj_bank["free"][par]
            k.wait("tensor", sem, val)
            if blk == 0 and mi == 0:
                k.wait("tensor", s_ld, ld_small)
            for dt in range(NDT):
                k.op("tensor", lambda t, w=wsb, dt=dt, b=bank: t.matmul(
                    b[:], w[:, dt * 128:(dt + 1) * 128],
                    xnT[:, dt * 512:(dt + 1) * 512],
                    start=(dt == 0), stop=(dt == NDT - 1)),
                    inc=(s_t if dt == NDT - 1 else None))
            mm_done = s_t.n
            if mname == "qk":
                k.wait("vector", s_t, mm_done)
                k.op("vector", lambda v, b=bank, sl=sl: v.tensor_copy(qkf[:, sl], b[:]),
                     inc=s_v)
                proj_bank["free"][par] = (s_v, s_v.n)
            elif mname == "v":
                k.wait("vector", s_t, mm_done)
                k.op("vector", lambda v, b=bank, sl=sl: v.tensor_copy(vfm[:, sl], b[:]),
                     inc=s_v)
                proj_bank["free"][par] = (s_v, s_v.n)
            elif mname == "bd":
                k.wait("scalar", s_t, mm_done)
                k.wait("scalar", s_v, bias_ready)
                k.wait("scalar", s_a, s_a.n)
                k.op("scalar", lambda s, b=bank: s.activation(
                    escr[0:64, 0:512], b[0:64, :], AF.Exp, bias=bias1[0:64, :], scale=-1.0),
                    inc=s_a)
                k.wait("scalar", s_a, s_a.n)
                k.op("scalar", lambda s, sl=sl: s.activation(
                    nlb2[0:64, sl], escr[0:64, 0:512], AF.Ln, bias=1.0), inc=s_a)
                k.wait("scalar", s_a, s_a.n)
                k.op("scalar", lambda s, b=bank: s.activation(
                    escr[64:128, 0:512], b[64:128, :], AF.Exp, bias=bias1[64:128, :],
                    scale=1.0), inc=s_a)
                k.wait("scalar", s_a, s_a.n)
                k.op("scalar", lambda s, sl=sl: s.activation(
                    dd[64:128, sl], escr[64:128, 0:512], AF.Ln, bias=1.0), inc=s_a)
                proj_bank["free"][par] = (s_a, s_a.n)
            else:
                k.wait("scalar", s_t, mm_done)
                k.wait("scalar", s_a, s_a.n)
                k.op("scalar", lambda s, b=bank: s.activation(
                    escr[64:128, 0:512], b[64:128, :], AF.Exp, bias=bias2[64:128, :],
                    scale=-1.0), inc=s_a)
                k.wait("scalar", s_a, s_a.n)
                k.op("scalar", lambda s, sl=sl: s.activation(
                    nlb2[64:128, sl], escr[64:128, 0:512], AF.Ln, bias=1.0), inc=s_a)
                proj_bank["free"][par] = (s_a, s_a.n)

    pair_info = {}
    mt_bank = {"free": [(s_v, 0), (s_v, 0)]}
    oT_free = {"m": (s_v, 0)}
    pP_free = {"m": (s_v, 0)}
    chain = {"ready": 0, "exps_done": 0}

    def emit_pair_indep(p, is_own):
        cols = slice(p * 128, (p + 1) * 128)
        ring = p % 2
        rsl = slice(ring * 128, (ring + 1) * 128)
        tm_ev = emit_transpose(kpf[:, cols], kptm[:, cols], 128)
        kpf0_ev = None
        if is_own:
            k.wait("tensor", tm_ev[0], tm_ev[1])
            kpf0_ev = emit_transpose(kptm[:, p * 128 + 64:p * 128 + 128], kpf0[0:64, cols], 64)
        vtm_ev = emit_transpose(vfm[:, cols], vtm[:, cols], 128)
        mt_ev = None
        if is_own:
            par = p % 2
            bank = pM0 if par == 0 else pM1
            sem, val = mt_bank["free"][par]
            k.wait("tensor", sem, val)
            k.wait("tensor", kpf0_ev[0], kpf0_ev[1])
            k.op("tensor", lambda t, b=bank: t.matmul(b[:, 0:128], kpf0[0:64, cols],
                                                      qp[0:64, cols], start=True, stop=True),
                 inc=s_t)
            mm_done = s_t.n
            k.wait("vector", s_t, mm_done)
            k.wait("vector", s_cst, consts_ld)
            k.op("vector", lambda v, b=bank: v.tensor_tensor(mtm[:, rsl], b[:, 0:128],
                                                             msk[:], OP.mult), inc=s_v)
            mt_bank["free"][par] = (s_v, s_v.n)
            mt_ev = s_v.n
        pair_info[p] = dict(mt_ev=mt_ev, tm_ev=tm_ev, vtm_ev=vtm_ev)

    def emit_pair_dep(p, is_own):
        cols = slice(p * 128, (p + 1) * 128)
        ring = p % 2
        info = pair_info[p]
        vsem, vval = info["vtm_ev"]
        tsem, tval = info["tm_ev"]
        if is_own:
            sem, val = oT_free["m"]
            k.wait("tensor", sem, val)
            k.wait("tensor", vsem, vval)
            k.wait("tensor", s_v, info["mt_ev"])
            k.op("tensor", lambda t: t.matmul(pOT[:, 0:128], vtm[:, cols],
                                              mtm[:, ring * 128:ring * 128 + 128],
                                              start=True, stop=False))
        for half in range(2):
            hsl = slice(p * 128 + half * 64, p * 128 + (half + 1) * 64)
            r0, r1 = half * 64, (half + 1) * 64
            if is_own:
                k.wait("tensor", s_st, chain["ready"])
                k.op("tensor", lambda t, hsl=hsl, half=half: t.matmul(
                    pOT[:, half * 64:half * 64 + 64], Sb[:], qp[0:64, hsl],
                    start=False, stop=(half == 1)), inc=(s_t if half == 1 else None))
            sem, val = pP_free["m"]
            k.wait("tensor", sem, val)
            k.wait("tensor", vsem, vval)
            k.wait("tensor", tsem, tval)
            if not is_own:
                k.wait("tensor", s_st, chain["ready"])
            k.op("tensor", lambda t, r0=r0, r1=r1: t.matmul(
                pP[0:64, 0:128], kptm[r0:r1, p * 128 + 64:p * 128 + 128],
                vtm[r0:r1, cols], start=True, stop=True), inc=s_t)
            pmm = s_t.n
            k.wait("vector", s_t, pmm)
            k.wait("vector", s_a, chain["exps_done"])
            k.op("vector", lambda v: v.tensor_tensor(Stmp[:], Sf[:], pP[0:64, 0:128], OP.add),
                 inc=s_v)
            pP_free["m"] = (s_v, s_v.n)
            k.wait("vector", s_v, s_v.n)
            k.op("vector", lambda v, p=p, half=half: v.tensor_scalar(
                Sf[:], Stmp[:], acb[:, 2 * p + half:2 * p + half + 1], None, OP.mult),
                 inc=s_v)
            k.wait("vector", s_v, s_v.n)
            k.op("vector", lambda v: v.tensor_copy(Sb[:], Sf[:]), inc=s_st)
            chain["ready"] = s_st.n
        if is_own:
            ot_done = s_t.n
            k.wait("vector", s_t, ot_done)
            k.op("vector", lambda v: v.tensor_copy(oT[:, cols], pOT[:, 0:128]), inc=s_v)
            oT_free["m"] = (s_v, s_v.n)

    xapply = {}
    xnb_free = [(s_v, 0), (s_v, 0)]

    def emit_segment(x_src, is_own):
        for tau in range(NT):
            par = tau % 2
            if tau >= 2:
                k.wait("scalar", s_a, xapply[tau - 2])
            k.dma("scalar", xt[:, par * D:(par + 1) * D],
                  x_src[tau * 128:(tau + 1) * 128, :], s_xt[par])
            ld = s_xt[par].n
            k.wait("vector", s_xt[par], ld)
            k.wait("scalar", s_xt[par], ld)
            a_done = emit_ln_stats_apply(xt[:, par * D:(par + 1) * D],
                                         xnb[:, par * D:(par + 1) * D],
                                         dst_free=xnb_free[par])
            xapply[tau] = a_done
            k.wait("tensor", s_a, a_done)
            bi = tau % 4
            for dt in range(NDT):
                emit_transpose(xnb[:, par * D + dt * 128:par * D + (dt + 1) * 128],
                               xnT[:, dt * 512 + bi * 128:dt * 512 + bi * 128 + 128], 128,
                               evac_eng=("vector" if dt % 2 == 0 else "scalar"))
            xnb_free[par] = (s_t, s_t.n)
            if bi == 3:
                k.wait("tensor", s_v, s_v.n)
                k.wait("tensor", s_a, s_a.n)
                emit_proj_block(tau // 4, is_own)
        k.wait("vector", s_a, s_a.n)
        k.wait("vector", s_v, s_v.n)
        for ch in range(2 * NP):
            k.op("vector", lambda v, o=ch * 64: v.tensor_tensor_scan(
                scum[:, o:o + 64], nlb2[:, o:o + 64], nlb2[:, o:o + 64],
                0.0, OP.add, OP.bypass), inc=(s_v if ch == 2 * NP - 1 else None))
        scan_done = s_v.n
        k.wait("scalar", s_v, scan_done)
        if is_own:
            k.op("scalar", lambda s: s.activation(aa[0:64, :], scum[0:64, :], AF.Exp,
                                                  scale=-1.0))
        k.op("scalar", lambda s: s.activation(aa[64:128, :], scum[64:128, :], AF.Exp,
                                              scale=1.0))
        scum_last = scum.rearrange("p (ch c) -> p ch c", c=64)
        k.op("scalar", lambda s: s.activation(acb[:, 0:32], scum_last[0:64, :, 63], AF.Exp,
                                              scale=-1.0), inc=s_a)
        exps_done = s_a.n
        chain["exps_done"] = exps_done
        k.wait("vector", s_a, exps_done)
        if is_own:
            k.op("vector", lambda v: v.tensor_tensor(qp[0:64, :], qkf[0:64, :],
                                                     aa[0:64, :], OP.mult))
        k.op("vector", lambda v: v.tensor_tensor(kpw[64:128, :], qkf[64:128, :],
                                                 dd[64:128, :], OP.mult), inc=s_v)
        k.wait("vector", s_v, s_v.n)
        k.op("vector", lambda v: v.tensor_tensor(kpf[64:128, :], kpw[64:128, :],
                                                 aa[64:128, :], OP.mult), inc=s_v)
        elt_done = s_v.n
        k.wait("tensor", s_v, elt_done)
        for p in range(NP + 1):
            if p < NP:
                emit_pair_indep(p, is_own)
            if p > 0:
                emit_pair_dep(p - 1, is_own)

    k.wait("vector", s_v, memset_done)
    emit_segment(xp_d, is_own=False)
    emit_segment(xo_d, is_own=True)

    x1_written = []
    att_ready = 0
    a4_xt = {}
    for tau in range(NT):
        par = tau % 2
        for nb in range(4):
            bpar = nb % 2
            bank = pA0 if bpar == 0 else pA1
            sem, val = proj_bank["free"][bpar]
            k.wait("tensor", sem, val)
            if tau == 0 and nb == 0:
                sem2, val2 = oT_free["m"]
                k.wait("tensor", sem2, val2)
            k.op("tensor", lambda t, nb=nb, tau=tau, b=bank: t.matmul(
                b[:], oT[:, tau * 128:(tau + 1) * 128], wo[:, nb * 512:(nb + 1) * 512],
                start=True, stop=True), inc=s_t)
            mm_done = s_t.n
            k.wait("vector", s_t, mm_done)
            if att_ready:
                k.wait("vector", s_a, att_ready)
            else:
                k.wait("vector", s_a, chain["exps_done"])
            k.op("vector", lambda v, nb=nb, b=bank: v.tensor_copy(
                att[:, nb * 512:(nb + 1) * 512], b[:]), inc=s_v)
            proj_bank["free"][bpar] = (s_v, s_v.n)
        att_done = s_v.n
        if tau >= 2:
            k.wait("scalar", s_v, a4_xt[tau - 2])
        else:
            k.wait("scalar", s_a, xapply[NT - 2 + tau])
        k.dma("scalar", xt[:, par * D:(par + 1) * D], xo_d[tau * 128:(tau + 1) * 128, :],
              s_xt[par])
        xld = s_xt[par].n
        k.wait("vector", s_v, att_done)
        k.wait("scalar", s_v, att_done)
        if x1_written:
            k.wait("scalar", s_x1w, x1_written[-1])
        a_done = emit_ln_stats_apply(att[:, 0:D], x1t[:, 0:D])
        att_ready = a_done
        k.wait("vector", s_a, a_done)
        k.wait("vector", s_xt[par], xld)
        k.op("vector", lambda v, par=par: v.tensor_tensor(
            x1t[:, 0:D], x1t[:, 0:D], xt[:, par * D:(par + 1) * D], OP.add), inc=s_v)
        x1_done = s_v.n
        a4_xt[tau] = x1_done

        k.wait("scalar", s_v, x1_done)
        k.dma("scalar", x1s[tau * 128:(tau + 1) * 128, :], x1t[:, 0:D], s_x1w)
        x1_written.append(s_x1w.n)
        k.wait("vector", s_v, x1_done)
        k.wait("scalar", s_v, x1_done)
        a2_done = emit_ln_stats_apply(x1t[:, 0:D], xnb[:, par * D:(par + 1) * D],
                                      dst_free=xnb_free[par])
        k.wait("tensor", s_a, a2_done)
        for dt in range(NDT):
            emit_transpose(xnb[:, par * D + dt * 128:par * D + (dt + 1) * 128],
                           xn2T[:, dt * TH + tau * 128:dt * TH + tau * 128 + 128], 128,
                           evac_eng=("vector" if dt % 2 == 0 else "scalar"))
        xnb_free[par] = (s_t, s_t.n)

    x1_all_written = s_x1w.n
    xn2T_v, xn2T_a = s_v.n, s_a.n

    k.barrier()
    al.free_from(amark)
    bmark = al.mark()
    w1n = al.sb("w1n_b", [128, 2 * D], BF16)
    w3n = al.sb("w3n_b", [128, 2 * D], BF16)
    w1t = al.sb("w1t_b", [128, 2 * D], BF16)
    w3t = al.sb("w3t_b", [128, 2 * D], BF16)
    hid = al.sb("hid_b", [128, 2 * 512], BF16)
    zbr = al.sb("zbr_b", [128, 2 * TH], BF16)

    w1r = w1_d.rearrange("(h1 j c) d -> c j h1 d", h1=64, j=2, c=64)
    w3r = w3_d.rearrange("(h1 j c) d -> c j h1 d", h1=64, j=2, c=64)
    import collections as _c
    wtq = {"dt": _c.defaultdict(int), "done": {}}

    def emit_wtrans(c):
        par = c % 2
        dt = wtq["dt"][c]
        wtq["dt"][c] += 1
        emit_transpose(w1n[:, par * D + dt * 128:par * D + (dt + 1) * 128],
                       w1t[:, par * D + dt * 128:par * D + (dt + 1) * 128], 128,
                       evac_eng=("vector" if dt % 2 == 0 else "scalar"))
        emit_transpose(w3n[:, par * D + dt * 128:par * D + (dt + 1) * 128],
                       w3t[:, par * D + dt * 128:par * D + (dt + 1) * 128], 128,
                       evac_eng=("scalar" if dt % 2 == 0 else "vector"))
        if wtq["dt"][c] == NDT:
            wtq["done"][c] = (s_t, s_t.n)

    def emit_wload(c):
        par = c % 2
        dsl = slice(par * D, (par + 1) * D)
        if c >= 2:
            sem, val = wtq["done"][c - 2]
            k.wait("gpsimd", sem, val)
        k.dma("gpsimd", w1n[:, dsl], w1r[c], s_wl[par])
        k.dma("gpsimd", w3n[:, dsl], w3r[c], s_wl[par])

    emit_wload(0)
    emit_wload(1)

    wtrans_done = []
    zb_written = []
    zb_count = [0, 0]
    bst = {"ug": [(s_v, 0), (s_v, 0)], "ul": [(s_v, 0), (s_v, 0)], "zb": (s_v, 0),
           "hid": [(s_v, 0), (s_v, 0)], "pend_bp": []}

    def emit_bp(job):
        hsl, tsl, par, mul_done, hpar = job
        sem, val = bst["zb"]
        k.wait("tensor", sem, val)
        k.wait("tensor", s_v, mul_done)
        k.op("tensor", lambda t, hsl=hsl: t.matmul(pM1[:], hb1[:], hid[:, hsl],
                                                   start=True, stop=True), inc=s_t)
        zb_mm = s_t.n
        bst["hid"][hpar] = (s_t, zb_mm)
        k.wait("vector", s_t, zb_mm)
        k.op("vector", lambda v, tsl=tsl, par=par: v.tensor_copy(
            zbr[:, par * TH + tsl.start:par * TH + tsl.stop], pM1[:]), inc=s_v)
        bst["zb"] = (s_v, s_v.n)
        zb_count[par] += 1
        if zb_count[par] % 4 == 0:
            cc = (zb_count[par] - 1) // 4 * 2 + par
            k.wait("sync", s_v, s_v.n)
            k.dma("sync", zbs[:, cc, :], zbr[:, par * TH:(par + 1) * TH], s_zbw[par])
            zb_written[cc] = s_zbw[par].n

    for c in range(NC):
        par = c % 2
        if c == 0:
            k.wait("tensor", s_wl[0], s_wl[0].n)
            k.wait("tensor", s_v, xn2T_v)
            k.wait("tensor", s_a, xn2T_a)
        while wtq["dt"][c] < NDT:
            emit_wtrans(c)
        if c == 0:
            emit_wload(2)
        k.wait("tensor", s_v, s_v.n)
        k.wait("tensor", s_a, s_a.n)
        if c >= 2:
            assert zb_written[c - 2] is not None
            k.wait("vector", s_zbw[par], zb_written[c - 2])

        for ts in range(4):
            tsl = slice(ts * 512, (ts + 1) * 512)
            hpar = ts % 2
            hsl = slice(hpar * 512, (hpar + 1) * 512)
            ugb = pA0 if hpar == 0 else pM0
            ulb = pA1 if hpar == 0 else pOT
            sem, val = bst["ug"][hpar]
            k.wait("tensor", sem, val)
            for dt in range(NDT):
                k.op("tensor", lambda t, dt=dt, ts=ts, par=par, b=ugb: t.matmul(
                    b[:], w1t[:, par * D + dt * 128:par * D + (dt + 1) * 128],
                    xn2T[:, dt * TH + ts * 512:dt * TH + ts * 512 + 512],
                    start=(dt == 0), stop=(dt == NDT - 1)),
                    inc=(s_t if dt == NDT - 1 else None))
            ug_done = s_t.n
            sem, val = bst["ul"][hpar]
            k.wait("tensor", sem, val)
            for dt in range(NDT):
                k.op("tensor", lambda t, dt=dt, ts=ts, par=par, b=ulb: t.matmul(
                    b[:], w3t[:, par * D + dt * 128:par * D + (dt + 1) * 128],
                    xn2T[:, dt * TH + ts * 512:dt * TH + ts * 512 + 512],
                    start=(dt == 0), stop=(dt == NDT - 1)),
                    inc=(s_t if dt == NDT - 1 else None))
            ul_done = s_t.n
            if c + 1 < NC:
                if wtq["dt"][c + 1] == 0:
                    k.wait("tensor", s_wl[(c + 1) % 2], s_wl[(c + 1) % 2].n)
                for _ in range(4):
                    if wtq["dt"][c + 1] < NDT:
                        emit_wtrans(c + 1)
                if wtq["dt"][c + 1] == NDT and c + 3 < NC:
                    emit_wload(c + 3)
            sem, val = bst["hid"][hpar]
            k.wait("scalar", sem, val)
            k.wait("scalar", s_t, ug_done)
            k.op("scalar", lambda s, hsl=hsl, b=ugb: s.activation(hid[:, hsl], b[:], AF.Silu),
                 inc=s_a)
            silu_done = s_a.n
            bst["ug"][hpar] = (s_a, silu_done)
            k.wait("vector", s_a, silu_done)
            k.wait("vector", s_t, ul_done)
            k.op("vector", lambda v, hsl=hsl, b=ulb: v.tensor_tensor(hid[:, hsl], hid[:, hsl],
                                                                     b[:], OP.mult), inc=s_v)
            mul_done = s_v.n
            bst["ul"][hpar] = (s_v, mul_done)
            bst["pend_bp"].append((hsl, tsl, par, mul_done, hpar))
            if len(bst["pend_bp"]) > 1:
                emit_bp(bst["pend_bp"].pop(0))
        wtrans_done.append((s_t, s_t.n))
        zb_written.append(None)
        if c == NC - 1:
            while bst["pend_bp"]:
                emit_bp(bst["pend_bp"].pop(0))

    k.barrier()
    al.free_from(bmark)
    al.free_from(xn2T_mark)
    al.free_from(aw_mark)
    cmark = al.mark()
    ZCH = 4
    GR = 4
    gbuf = al.sb("g_b", [128, GR * TH], BF16)
    zfr = al.sb("zfr_b", [128, 2 * TH], BF16)
    zfl = al.sb("zfl_b", [128, 2 * ZCH * 512], BF16)
    w2t = al.sb("w2t_b", [128, 2 * 64 * 512], BF16)
    otb = al.sb("otb_b", [128, 4 * 512], BF16)

    zb_view = zbs.rearrange("(j q) c t -> q j c t", j=2)
    zfs_view = zfs.rearrange("g p t -> p g t")

    gfree = [(s_v, 0)] * GR
    zfr_free = [(s_v, 0)] * 2
    banks4 = (pA0, pA1, pM0, pOT)
    cbank = [(s_v, 0)] * 4
    k.wait("sync", s_zbw[0], s_zbw[0].n)
    k.wait("sync", s_zbw[1], s_zbw[1].n)
    for g1 in range(64):
        r = g1 % GR
        r2 = g1 % 2
        gb = slice(r * TH, (r + 1) * TH)
        zb_ = slice(r2 * TH, (r2 + 1) * TH)
        sem, val = gfree[r]
        k.wait("sync", sem, val)
        k.dma("sync", gbuf[:, gb], zb_view[g1], s_gg[r])
        gld = s_gg[r].n
        psem, pval = zfr_free[r2]
        k.wait("vector", psem, pval)
        k.wait("tensor", s_gg[r], gld)
        for i in range(4):
            bank = banks4[i]
            sem, val = cbank[i]
            k.wait("tensor", sem, val)
            k.op("tensor", lambda t, o=gb.start + i * 512, b=bank: t.matmul(
                b[:], hb2[:], gbuf[:, o:o + 512], start=True, stop=True), inc=s_t)
            cmms = s_t.n
            k.wait("vector", s_t, cmms)
            k.op("vector", lambda v, o=zb_.start + i * 512, b=bank: v.tensor_scalar(
                zfr[:, o:o + 512], b[:], RS, None, OP.mult), inc=s_v)
            cbank[i] = (s_v, s_v.n)
        gfree[r] = (s_t, s_t.n)
        ev = s_v.n
        k.wait("gpsimd", s_v, ev)
        k.dma("gpsimd", zfs[g1], zfr[:, zb_], s_zfw[g1 % 4])
        zfr_free[r2] = (s_zfw[g1 % 4], s_zfw[g1 % 4].n)

    banksC = (pM0, pOT, pP, pM1)
    w2t_ld = [None] * 4
    grp_mm_done = [None] * 4

    def emit_w2t_loads(grp):
        gpar = grp % 2
        base = gpar * 64 * 512
        if grp >= 2:
            sem, val = grp_mm_done[grp - 2]
            k.wait("scalar", sem, val)
        if grp == 0:
            k.wait("scalar", s_w2c, w2cast_done)
        for a1 in range(64):
            k.dmat("scalar", w2t[:, base + a1 * 512:base + (a1 + 1) * 512],
                   w2s[grp * 512:(grp + 1) * 512, a1 * 128:(a1 + 1) * 128], s_w2t[gpar])
        w2t_ld[grp] = s_w2t[gpar].n

    emit_w2t_loads(0)
    emit_w2t_loads(1)

    zfl_free = [(s_v, 0)] * 2
    cmb_free = [cbank[2], cbank[3], (s_v, 0), (s_v, 0)]
    otb_gate = 0
    for r in range(4):
        k.wait("sync", s_zfw[r], s_zfw[r].n)
    ffs_dT_view = ffs_dT.rearrange("(a p) t -> p a t", p=128)
    for grp in range(4):
        gpar = grp % 2
        base = gpar * 64 * 512
        k.wait("tensor", s_w2t[grp % 2], w2t_ld[grp])
        for ts in range(4):
            for bi in range(4):
                sem, val = cmb_free[bi]
                k.wait("tensor", sem, val)
            for zc in range(64 // ZCH):
                r = zc % 2
                a1a = zc * ZCH
                base_r = r * ZCH * 512
                sem, val = zfl_free[r]
                k.wait("sync", sem, val)
                k.dma("sync", zfl[:, base_r:base_r + ZCH * 512].rearrange(
                    "p (q t) -> p q t", q=ZCH),
                    zfs_view[:, a1a:a1a + ZCH, ts * 512:(ts + 1) * 512], s_zfl[r])
                zld = s_zfl[r].n
                k.wait("tensor", s_zfl[r], zld)
                for i in range(ZCH):
                    a1 = a1a + i
                    for dt in range(4):
                        k.op("tensor", lambda t, a1=a1, dt=dt, o=base_r + i * 512,
                             base=base: t.matmul(
                            banksC[dt][:],
                            w2t[:, base + a1 * 512 + dt * 128:base + a1 * 512 + (dt + 1) * 128],
                            zfl[:, o:o + 512], start=(a1 == 0), stop=(a1 == 63)),
                            inc=(s_t if dt == 3 else None))
                zfl_free[r] = (s_t, s_t.n)
            acc_done = s_t.n
            k.wait("vector", s_t, acc_done)
            if otb_gate:
                k.wait("vector", s_ffw, otb_gate)
            for dt in range(4):
                k.op("vector", lambda v, dt=dt: v.tensor_copy(
                    otb[:, dt * 512:(dt + 1) * 512], banksC[dt][:]), inc=s_v)
                cmb_free[dt] = (s_v, s_v.n)
            evd = s_v.n
            k.wait("sync", s_v, evd)
            k.dma("sync", ffs_dT_view[:, grp * 4:(grp + 1) * 4, ts * 512:(ts + 1) * 512],
                  otb[:].rearrange("p (q t) -> p q t", q=4), s_ffw)
            otb_gate = s_ffw.n
        grp_mm_done[grp] = (s_t, s_t.n)
        if grp + 2 < 4:
            emit_w2t_loads(grp + 2)
    ffs_all = s_ffw.n

    k.barrier()
    al.free_from(cmark)
    fft = al.sb("fft_b", [128, 2 * D], BF16)
    out_res = []
    out_dma = []
    for tau in range(NT):
        par = tau % 2
        dsl = slice(par * D, (par + 1) * D)
        if tau == 0:
            k.wait("scalar", s_ffw, ffs_all)
            k.wait("scalar", s_x1w, x1_all_written)
        if tau >= 2:
            k.wait("scalar", s_v, out_res[tau - 2])
        k.dmat("scalar", fft[:, dsl], ffs_dT[:, tau * 128:(tau + 1) * 128], s_ffl[par])
        k.dma("scalar", xt[:, dsl], x1s[tau * 128:(tau + 1) * 128, :], s_ffl[par])
        ld = s_ffl[par].n
        if tau >= 1:
            k.wait("scalar", s_out, out_dma[tau - 1])
        k.wait("vector", s_ffl[par], ld)
        k.wait("scalar", s_ffl[par], ld)
        a_done = emit_ln_stats_apply(fft[:, dsl], x1t[:, 0:D])
        k.wait("vector", s_a, a_done)
        k.op("vector", lambda v, dsl=dsl: v.tensor_tensor(x1t[:, 0:D], x1t[:, 0:D],
                                                          xt[:, dsl], OP.add), inc=s_v)
        out_res.append(s_v.n)
        k.wait("sync", s_v, s_v.n)
        k.dma("sync", out_d[tau * 128:(tau + 1) * 128, :], x1t[:, 0:D], s_out)
        out_dma.append(s_out.n)
    k.wait("sync", s_out, s_out.n)

    with nc.Block() as block:
        k.build(block)
    k.close_sems()
    for ctx in reversed(psum_ctxs):
        ctx.__exit__(None, None, None)
    al.free_from(0)
    return nc


def _consts():
    idx = np.arange(64)
    pc = np.zeros((64, 64), dtype=np.int64)
    for i in range(64):
        pc[i] = [bin(i & j).count("1") for j in range(64)]
    H64 = np.where(pc % 2 == 0, 1.0, -1.0).astype(np.float32)
    H2 = np.array([[1.0, 1.0], [1.0, -1.0]], dtype=np.float32)
    hb1 = np.einsum("ab,cd->acbd", H2, H64).reshape(128, 128)
    hb2 = np.zeros((128, 128), dtype=np.float32)
    hb2[0:64, 0:64] = H64
    hb2[64:128, 64:128] = H64
    tri = np.triu(np.ones((64, 64), dtype=np.float32))
    mask = np.zeros((128, 128), dtype=np.float32)
    mask[0:64, 0:64] = tri
    mask[64:128, 64:128] = tri
    bf = ml_dtypes.bfloat16
    return (hb1.astype(bf), hb2.astype(bf), mask.astype(bf), np.eye(128, dtype=bf))


def kernel(L0_attn_x, L0_attn_Wq, L0_attn_Wk, L0_attn_Wv, L0_attn_Wo,
           L0_attn_Wb, L0_attn_bb, L0_attn_Wd, L0_attn_bd,
           L0_ffn_W1, L0_ffn_W2, L0_ffn_W3):
    x = np.ascontiguousarray(np.asarray(L0_attn_x, dtype=np.float32))
    wqk = np.ascontiguousarray(np.concatenate([L0_attn_Wq, L0_attn_Wk], axis=1).astype(np.float32))
    wbd = np.ascontiguousarray(np.concatenate([L0_attn_Wb, L0_attn_Wd], axis=1).astype(np.float32))
    wbb = np.ascontiguousarray(np.concatenate([L0_attn_Wb, L0_attn_Wb], axis=1).astype(np.float32))
    wv = np.ascontiguousarray(np.asarray(L0_attn_Wv, dtype=np.float32))
    wo = np.ascontiguousarray(np.asarray(L0_attn_Wo, dtype=np.float32))
    bb = np.ascontiguousarray(np.asarray(L0_attn_bb, dtype=np.float32).reshape(64, 1))
    bd = np.ascontiguousarray(np.asarray(L0_attn_bd, dtype=np.float32).reshape(64, 1))
    w1 = np.ascontiguousarray(np.asarray(L0_ffn_W1, dtype=np.float32))
    w2 = np.ascontiguousarray(np.asarray(L0_ffn_W2, dtype=np.float32))
    w3 = np.ascontiguousarray(np.asarray(L0_ffn_W3, dtype=np.float32))
    hb1, hb2, mask, idn = _consts()
    zeros = np.zeros((TH, D), dtype=np.float32)

    nc = build_nc()
    in_maps = []
    for core in range(8):
        b, half = core // 2, core % 2
        xo = np.ascontiguousarray(x[b, half * TH:(half + 1) * TH, :])
        xp = np.ascontiguousarray(x[b, 0:TH, :]) if half == 1 else zeros
        in_maps.append({
            "xo": xo, "xp": xp, "wqk": wqk, "wbd": wbd, "wbb": wbb, "wv": wv,
            "wo": wo, "bb": bb, "bd": bd, "w1": w1, "w2": w2, "w3": w3,
            "hb1": hb1, "hb2": hb2, "msk": mask, "idn": idn,
        })
    res = run_bass_kernel_spmd(nc, in_maps, core_ids=list(range(8)))
    out = np.empty((4, 4096, 2048), dtype=np.float32)
    for core in range(8):
        b, half = core // 2, core % 2
        out[b, half * TH:(half + 1) * TH, :] = res.results[core]["out"]
    return out
